# version 04
# baseline (speedup 1.0000x reference)
import math
import numpy as np
import ml_dtypes
import concourse.bass as bass
import concourse.mybir as mybir
from concourse.bass_utils import run_bass_kernel_spmd

F32 = mybir.dt.float32
BF16 = mybir.dt.bfloat16
I32 = mybir.dt.int32
Alu = mybir.AluOpType
Act = mybir.ActivationFunctionType
AxX = mybir.AxisListType.X

NCORES = 8
TCORE = 4096
TT = 512
NT = TCORE // TT
D = 1024
DI = 2048
DS = 16
DFF = 4096
NCT = DI // 128
KD = D // 128
KFF = DFF // 128
UPC = TT + 4
INV2PI = float(1.0 / (2.0 * math.pi))
TWOPI = float(2.0 * math.pi)

DEBUG_TAPS = False

ENGINES = ("sync", "scalar", "vector", "tensor", "gpsimd")


class Prog:

    def __init__(self):
        self.ops = {e: [] for e in ENGINES}
        self.cnt = {}

    def emit(self, eng, fn, waits=(), inc=None, incby=1):
        ws = [(s, v) for (s, v) in waits if v > 0]
        self.ops[eng].append((fn, ws, inc, incby))
        if inc is not None:
            self.cnt[inc] = self.cnt.get(inc, 0) + incby

    def cur(self, sem):
        return self.cnt.get(sem, 0)

    def snap(self):
        return [(s, v) for s, v in self.cnt.items()]

    def realize(self, block, sems):
        for engname in ENGINES:
            ops = self.ops[engname]
            if not ops:
                continue

            def make(engops):
                def _f(eng):
                    waited = {}
                    for fn, ws, inc, incby in engops:
                        for (sn, v) in ws:
                            if waited.get(sn, -1) < v:
                                eng.wait_ge(sems[sn], v)
                                waited[sn] = v
                        instr = fn(eng)
                        if inc is not None:
                            instr.then_inc(sems[inc], incby)
                return _f

            getattr(block, engname)(make(ops))



def build_fused():
    nc = bass.Bass("TRN2", target_bir_lowering=False)
    dt_in = lambda n, s, d=F32: nc.dram_tensor(n, s, d, kind="ExternalInput").ap()
    dt_out = lambda n, s, d=F32: nc.dram_tensor(n, s, d, kind="ExternalOutput").ap()
    dt_int = lambda n, s, d=F32: nc.dram_tensor(n, s, d).ap()

    x_h = dt_in("x", [TCORE, D])
    xhalo_h = dt_in("xhalo", [128, D])
    Win_h = dt_in("W_in", [D, 2 * DI])
    Wdt_h = dt_in("W_dt", [DI, DI])
    Wx_h = dt_in("W_x16", [DI, DS])
    W1_h = dt_in("W1", [D, DFF])
    W2_h = dt_in("W2", [DFF, D])
    binh_h = dt_in("b_in_h", [128, NCT])
    binz_h = dt_in("b_in_z", [128, NCT])
    bdt_h = dt_in("b_dt", [128, NCT])
    bx_h = dt_in("b_x16", [128, 1])
    cw_h = dt_in("conv_w", [128, NCT * 4])
    cb_h = dt_in("conv_b", [128, NCT])
    Alog_h = dt_in("A_log", [128, NCT * DS])
    Dv_h = dt_in("Dv", [128, NCT])
    b1_h = dt_in("b1", [128, KFF])
    th_h = dt_in("theta", [128, KFF])
    wr_h = dt_in("wreal", [128, KFF])
    wi_h = dt_in("wimag", [128, KFF])
    b2_h = dt_in("b2", [128, KD])
    identbf_h = dt_in("identbf", [128, 128], BF16)
    identf_h = dt_in("identf", [128, 128])
    sels_h = dt_in("sels", [DS, DS * 128], BF16)
    hmask_h = dt_in("halo_mask", [128, 1])

    Wout_h = dt_in("W_out", [DI, D])
    bout_h = dt_in("b_out", [128, KD])
    pmask_h = dt_in("pmask", [128, 4])
    o_out = dt_out("o_out", [TCORE, D])
    exch_in_t = nc.dram_tensor("exch_in", [128, 2 * NCT * DS], F32)
    exch_all_t = nc.dram_tensor("exch_all", [4, 128, 2 * NCT * DS], F32)
    o_exch = exch_in_t.ap()
    exch_all = exch_all_t.ap()
    o_delta = dt_int("sp_delta", [NT, 128, NCT * TT], BF16)
    o_y1 = dt_int("sp_y1", [NT, 128, NCT * TT], BF16)
    o_siluz = dt_int("sp_siluz", [NT, 128, NCT * TT], BF16)
    o_acc = dt_int("sp_acc", [NT, 128, 4 * D])
    xn_sp = dt_int("xn_sp", [NT, 128, KD * TT], BF16)
    if DEBUG_TAPS:
        o_exch_t = dt_out("o_exch_t", [128, 2 * NCT * DS])
        o_B = dt_out("o_B", [NT, 16, TT])
        o_Bbc = dt_out("o_Bbc", [NT, 128, DS * TT], BF16)
        o_hc = dt_out("o_hc", [NT, 128, NCT * DS])
        o_u = dt_out("o_u", [NT, 128, NCT * TT], BF16)
        o_xn = dt_out("o_xn", [NT, 128, KD * TT], BF16)
        o_actq = dt_out("o_actq", [NT, 128, KFF * TT], BF16)
        o_ffn = dt_out("o_ffn", [NT, 128, KD * TT])

    from contextlib import ExitStack
    es = ExitStack()
    S = lambda n, sh, d=F32: es.enter_context(nc.sbuf_tensor("sb_" + n, sh, d))

    binh = S("binh", [128, NCT]); binz = S("binz", [128, NCT])
    bdt = S("bdt", [128, NCT]); bx = S("bx", [128, 1])
    cw = S("cw", [128, NCT * 4]); cb = S("cb", [128, NCT])
    Asb = S("Asb", [128, NCT * DS]); Dv = S("Dv", [128, NCT])
    b1 = S("b1", [128, KFF]); th = S("th", [128, KFF])
    wr = S("wr", [128, KFF]); wi = S("wi", [128, KFF])
    b2 = S("b2", [128, KD])
    identbf = S("identbf", [128, 128], BF16)
    identf = S("identf", [128, 128])
    sels = S("sels", [DS, DS * 128], BF16)
    hmask = S("hmask", [128, 1])
    c1 = S("c1", [128, KFF]); biasm = S("biasm", [128, KFF])
    hcar = S("hcar", [128, NCT * DS])
    Gend = S("Gend", [128, NCT])
    tA = S("tA", [128, KFF]); tB = S("tB", [128, KFF])
    tC = S("tC", [128, KFF]); tI = S("tI", [128, KFF], I32)
    aend = S("aend", [128, NCT * DS])
    lnc = S("lnc", [128, 16])
    Gtmp = S("Gtmp", [128, NCT])
    pmask = S("pmask", [128, 4])
    bout = S("bout", [128, KD])

    WA = S("WA", [128, 32768], BF16)
    AR1 = S("AR1", [128, 16448], BF16)
    AR2 = S("AR2", [128, 16384], BF16)
    AR3 = S("AR3", [128, 16384], BF16)
    AR4 = S("AR4", [128, 14848], BF16)
    xn_fm = S("xn_fm", [128, KD * TT], BF16)
    WXs = S("WXs", [128, NCT * DS], BF16)

    u_pre = AR1[:, 0:NCT * UPC]
    u_bf = AR1[:, 8256:8256 + NCT * TT]
    d_bf = AR2[:, 0:NCT * TT]
    y1_bf = AR2[:, NCT * TT:2 * NCT * TT]
    sz_bf = AR3[:, 0:NCT * TT]
    Bbc = AR3[:, NCT * TT:2 * NCT * TT]
    xn_tok = AR4[:, 0:4 * D]
    xtok_bf = AR4[:, 4 * D:8 * D]
    abuf = AR4[:, 8192:8704]
    bbuf = AR4[:, 8704:9216]
    hbuf = AR4[:, 9216:9728]
    du = AR4[:, 9728:10240]
    cbuf = AR4[:, 10240:11264].bitcast(F32)
    spsc = AR4[:, 10240:11264].bitcast(F32)
    Bsb = AR4[:, 11264:12288].bitcast(F32)
    Bsb_bf = AR4[:, 12288:12800]
    sqs = AR4[:, 12800:14848].bitcast(F32)
    actq = AR1[:, 0:KFF * TT]
    ffn_fm = AR2[:, 0:2 * KD * TT].bitcast(F32)
    xacc = AR2[:, 2 * KD * TT:4 * KD * TT].bitcast(F32)
    W2s = AR3[:, 0:2 * D]
    qm = AR3[:, 2048:3072].bitcast(F32)
    qh = AR3[:, 3072:4096].bitcast(F32)
    qf = AR3[:, 4096:5120].bitcast(F32)
    qw = AR3[:, 5120:6144].bitcast(F32)
    qs = AR3[:, 6144:7168].bitcast(F32)
    qi = AR3[:, 7168:8192].bitcast(I32)
    Wo = WA[:, 0:NCT * D]
    d_bf_c = AR2[:, 0:NCT * TT]
    y1_bf_c = AR2[:, NCT * TT:2 * NCT * TT]
    sz_bf_c = AR3[:, 0:NCT * TT]
    y2 = AR1[:, 0:NCT * TT]
    attn = AR1[:, NCT * TT:2 * NCT * TT].bitcast(F32)
    acc = AR3[:, NCT * TT:2 * NCT * TT].bitcast(F32)
    exg = AR4[:, 0:4096].bitcast(F32)
    Hst = AR4[:, 4096:4608].bitcast(F32)
    tmpc = AR4[:, 4608:5120].bitcast(F32)
    Gbuf = AR4[:, 5120:6144].bitcast(F32)
    y1f = AR4[:, 6144:7168].bitcast(F32)
    Ebuf = AR4[:, 7168:8192].bitcast(F32)
    Grun = AR4[:, 8192:8224].bitcast(F32)
    onesm = AR4[:, 8224:8232].bitcast(F32)

    sems = {}
    semstack = ExitStack()
    for name in ("dS", "dG", "sT", "sS", "sV", "sG"):
        sems[name] = semstack.enter_context(nc.semaphore(name))

    p = Prog()

    def ld(dst, src):
        p.emit("sync", lambda e, d=dst, s=src: e.dma_start(out=d, in_=s),
               inc="dS", incby=16)

    ld(binh[:], binh_h[:]); ld(binz[:], binz_h[:]); ld(bdt[:], bdt_h[:])
    ld(bx[:], bx_h[:]); ld(cw[:], cw_h[:]); ld(cb[:], cb_h[:])
    ld(Asb[:], Alog_h[:]); ld(Dv[:], Dv_h[:]); ld(b1[:], b1_h[:])
    ld(th[:], th_h[:]); ld(wr[:], wr_h[:]); ld(wi[:], wi_h[:])
    ld(b2[:], b2_h[:]); ld(identbf[:], identbf_h[:]); ld(identf[:], identf_h[:])
    ld(sels[:], sels_h[:]); ld(hmask[:], hmask_h[:])
    ld(pmask[:], pmask_h[:]); ld(bout[:], bout_h[:])
    W = p.snap()

    def vf(fn):
        p.emit("vector", fn, waits=[("sV", p.cur("sV")), ("sS", p.cur("sS"))], inc="sV")

    p.emit("scalar", lambda e: e.activation(Asb[:], Asb[:], Act.Exp), waits=W, inc="sS")
    vf(lambda e: e.tensor_scalar(Asb[:], Asb[:], -1.0, None, Alu.mult))
    vf(lambda e: e.tensor_scalar(tA[:], th[:], INV2PI, None, Alu.mult))
    vf(lambda e: e.tensor_copy(tI[:], tA[:]))
    vf(lambda e: e.tensor_copy(tC[:], tI[:]))
    vf(lambda e: e.tensor_tensor(tA[:], tA[:], tC[:], Alu.subtract))
    sv0 = p.cur("sV")
    p.emit("scalar", lambda e: e.activation(tA[:], tA[:], Act.Sin, scale=TWOPI),
           waits=[("sV", sv0)], inc="sS")
    vf(lambda e: e.tensor_tensor(tB[:], tA[:], wi[:], Alu.mult))
    vf(lambda e: e.tensor_scalar(tA[:], th[:], INV2PI, 0.25, Alu.mult, Alu.add))
    vf(lambda e: e.tensor_copy(tI[:], tA[:]))
    vf(lambda e: e.tensor_copy(tC[:], tI[:]))
    vf(lambda e: e.tensor_tensor(tA[:], tA[:], tC[:], Alu.subtract))
    sv1 = p.cur("sV")
    p.emit("scalar", lambda e: e.activation(tA[:], tA[:], Act.Sin, scale=TWOPI),
           waits=[("sV", sv1)], inc="sS")
    vf(lambda e: e.tensor_tensor(tA[:], tA[:], wr[:], Alu.mult))
    vf(lambda e: e.tensor_tensor(tA[:], tA[:], tB[:], Alu.add))
    vf(lambda e: e.tensor_tensor(tA[:], tA[:], wi[:], Alu.mult))
    vf(lambda e: e.tensor_scalar(c1[:], tA[:], 0.1, None, Alu.mult))
    vf(lambda e: e.tensor_scalar(tB[:], b1[:], 0.1 * INV2PI, None, Alu.mult))
    vf(lambda e: e.tensor_scalar(biasm[:], th[:], INV2PI, None, Alu.mult))
    vf(lambda e: e.tensor_tensor(biasm[:], biasm[:], tB[:], Alu.add))
    vf(lambda e: e.memset(hcar[:], 0.0))
    vf(lambda e: e.memset(Gend[:], 0.0))
    vf(lambda e: e.memset(AR1[:, 0:NCT * UPC], 0.0))

    PS = [es.enter_context(nc.psum_tensor(f"ps{i}", [128, TT], F32)) for i in range(8)]

    def mixer_tile(j):
        tt = 128 if j < 0 else TT
        ntb = tt // 128

        Wm = p.snap()
        for k in range(KD):
            p.emit("gpsimd", lambda e, k=k: e.dma_start(
                out=WA[:, k * (2 * DI):(k + 1) * (2 * DI)],
                in_=Win_h[k * 128:(k + 1) * 128, :]), waits=Wm, inc="dG", incby=16)
        if j < 0:
            p.emit("gpsimd", lambda e: e.dma_start(
                out=xtok_bf[:, 0:D], in_=xhalo_h[:]), waits=Wm, inc="dG", incby=16)
        else:
            for b in range(4):
                p.emit("gpsimd", lambda e, j=j, b=b: e.dma_start(
                    out=xtok_bf[:, b * D:(b + 1) * D],
                    in_=x_h[j * TT + b * 128: j * TT + (b + 1) * 128, :]),
                    waits=Wm, inc="dG", incby=16)
        Wx_ = p.snap()

        for b in range(ntb):
            xb = xtok_bf[:, b * D:(b + 1) * D]
            p.emit("scalar", lambda e, xb=xb, b=b: e.activation(
                sqs[:], xb, Act.Square, accum_out=lnc[:, b + 4:b + 5]),
                waits=Wx_, inc="sS")
            p.emit("vector", lambda e, xb=xb, b=b: e.reduce_sum(
                lnc[:, b:b + 1], xb, axis=AxX), waits=Wx_, inc="sV")
        def lf(eng, fn):
            p.emit(eng, fn, waits=[("sV", p.cur("sV")), ("sS", p.cur("sS"))],
                   inc="sV" if eng == "vector" else "sS")
        nb = ntb
        lf("vector", lambda e: e.tensor_scalar(
            lnc[:, 0:nb], lnc[:, 0:nb], 1.0 / D, None, Alu.mult))
        lf("vector", lambda e: e.tensor_scalar(
            lnc[:, 4:4 + nb], lnc[:, 4:4 + nb], 1.0 / D, None, Alu.mult))
        lf("vector", lambda e: e.tensor_tensor(
            lnc[:, 8:8 + nb], lnc[:, 0:nb], lnc[:, 0:nb], Alu.mult))
        lf("vector", lambda e: e.tensor_tensor(
            lnc[:, 4:4 + nb], lnc[:, 4:4 + nb], lnc[:, 8:8 + nb], Alu.subtract))
        lf("vector", lambda e: e.tensor_scalar(
            lnc[:, 4:4 + nb], lnc[:, 4:4 + nb], 1.0, 1e-5, Alu.mult, Alu.add))
        lf("scalar", lambda e: e.activation(lnc[:, 4:4 + nb], lnc[:, 4:4 + nb], Act.Ln))
        lf("scalar", lambda e: e.activation(
            lnc[:, 4:4 + nb], lnc[:, 4:4 + nb], Act.Exp, scale=-0.5))
        lf("vector", lambda e: e.tensor_tensor(
            lnc[:, 8:8 + nb], lnc[:, 0:nb], lnc[:, 4:4 + nb], Alu.mult))
        lf("vector", lambda e: e.tensor_scalar(
            lnc[:, 8:8 + nb], lnc[:, 8:8 + nb], -1.0, None, Alu.mult))
        sv = p.cur("sV")
        for b in range(ntb):
            p.emit("scalar", lambda e, b=b: e.activation(
                xn_tok[:, b * D:(b + 1) * D], xtok_bf[:, b * D:(b + 1) * D], Act.Identity,
                bias=lnc[:, b + 8:b + 9], scale=lnc[:, b + 4:b + 5]),
                waits=[("sV", sv)], inc="sS")
        Wn = p.snap()

        ps7bf = PS[7][:].bitcast(BF16)
        for fb in range(KD):
            for tb in range(ntb):
                tw = (Wn if fb == 0 else [("sV", p.cur("sV"))]) if tb == 0 else ()
                p.emit("tensor", lambda e, fb=fb, tb=tb, ps7bf=ps7bf: e.transpose(
                    ps7bf[:, tb * 128:(tb + 1) * 128],
                    xn_tok[:, tb * D + fb * 128: tb * D + (fb + 1) * 128],
                    identbf[:]), waits=tw, inc="sT")
            st = p.cur("sT")
            p.emit("vector", lambda e, fb=fb, ntb=ntb, ps7bf=ps7bf: e.tensor_copy(
                xn_fm[:, fb * TT: fb * TT + ntb * 128], ps7bf[:, 0:ntb * 128]),
                waits=[("sT", st)], inc="sV")
        Wt = p.snap()
        if j >= 0:
            p.emit("gpsimd", lambda e, j=j: e.dma_start(
                out=xn_sp[j], in_=xn_fm[:]), waits=Wt, inc="dG", incby=16)

        evs = p.cur("sS")
        for m in range(32):
            bank = PS[m % 2]
            mywait = list(Wt)
            if m >= 2:
                mywait.append(("sS", evs + (m - 1)))
            for k in range(KD):
                p.emit("tensor", lambda e, m=m, k=k, bank=bank, ntb=ntb: e.matmul(
                    bank[:, 0:ntb * 128],
                    WA[:, k * (2 * DI) + m * 128: k * (2 * DI) + (m + 1) * 128],
                    xn_fm[:, k * TT: k * TT + ntb * 128],
                    start=(k == 0), stop=(k == KD - 1)),
                    waits=mywait if k == 0 else (), inc="sT")
            st = p.cur("sT")
            if m < NCT:
                p.emit("scalar", lambda e, m=m, bank=bank, ntb=ntb: e.activation(
                    u_pre[:, m * UPC + 3: m * UPC + 3 + ntb * 128],
                    bank[:, 0:ntb * 128], Act.Identity, bias=binh[:, m:m + 1]),
                    waits=[("sT", st)], inc="sS")
            else:
                mz = m - NCT
                p.emit("scalar", lambda e, mz=mz, bank=bank, ntb=ntb: e.activation(
                    sz_bf[:, mz * TT: mz * TT + ntb * 128],
                    bank[:, 0:ntb * 128], Act.Silu, bias=binz[:, mz:mz + 1]),
                    waits=[("sT", st)], inc="sS")
        Wz = p.snap()

        if j < 0:
            for ct in range(NCT):
                p.emit("vector", lambda e, ct=ct: e.tensor_scalar(
                    u_pre[:, ct * UPC: ct * UPC + 3],
                    u_pre[:, ct * UPC + 128: ct * UPC + 131],
                    hmask[:, 0:1], None, Alu.mult), waits=Wz, inc="sV")
            return

        for k in range(NCT):
            p.emit("gpsimd", lambda e, k=k: e.dma_start(
                out=WA[:, k * DI:(k + 1) * DI],
                in_=Wdt_h[k * 128:(k + 1) * 128, :]), waits=Wz, inc="dG", incby=16)
        for k in range(NCT):
            p.emit("gpsimd", lambda e, k=k: e.dma_start(
                out=WXs[:, k * DS:(k + 1) * DS],
                in_=Wx_h[k * 128:(k + 1) * 128, :]), inc="dG", incby=16)
        Wwd = p.snap()

        for ct in range(NCT):
            base = ct * UPC
            p.emit("vector", lambda e, ct=ct, base=base: e.tensor_scalar(
                cbuf[:], u_pre[:, base:base + TT], cw[:, ct * 4:ct * 4 + 1], None, Alu.mult),
                waits=Wz if ct == 0 else (), inc="sV")
            for k in range(1, 4):
                p.emit("vector", lambda e, ct=ct, base=base, k=k: e.scalar_tensor_tensor(
                    cbuf[:], u_pre[:, base + k:base + k + TT],
                    cw[:, ct * 4 + k:ct * 4 + k + 1], cbuf[:], Alu.mult, Alu.add), inc="sV")
            sv = p.cur("sV")
            p.emit("scalar", lambda e, ct=ct: e.activation(
                u_bf[:, ct * TT:(ct + 1) * TT], cbuf[:], Act.Silu, bias=cb[:, ct:ct + 1]),
                waits=[("sV", sv)], inc="sS")
            ss = p.cur("sS")
            p.emit("vector", lambda e, base=base: e.tensor_scalar(
                u_pre[:, base:base + 3], u_pre[:, base + TT:base + TT + 3],
                1.0, None, Alu.mult), waits=[("sS", ss)], inc="sV")
        Wu = p.snap()

        evs = p.cur("sS")
        for m in range(NCT):
            bank = PS[2 + (m % 2)]
            mywait = Wu + Wwd
            if m >= 2:
                mywait = mywait + [("sS", evs + 2 * (m - 1))]
            for k in range(NCT):
                p.emit("tensor", lambda e, m=m, k=k, bank=bank: e.matmul(
                    bank[:], WA[:, k * DI + m * 128: k * DI + (m + 1) * 128],
                    u_bf[:, k * TT:(k + 1) * TT],
                    start=(k == 0), stop=(k == NCT - 1)),
                    waits=mywait if k == 0 else (), inc="sT")
            st = p.cur("sT")
            p.emit("scalar", lambda e, m=m, bank=bank: e.activation(
                spsc[:], bank[:], Act.Exp, bias=bdt[:, m:m + 1]),
                waits=[("sT", st)], inc="sS")
            p.emit("scalar", lambda e, m=m: e.activation(
                d_bf[:, m * TT:(m + 1) * TT], spsc[:], Act.Ln, bias=1.0), inc="sS")
        Wd = p.snap()

        for ct in range(NCT):
            p.emit("vector", lambda e, ct=ct: e.reduce_sum(
                Gtmp[:, ct:ct + 1], d_bf[:, ct * TT:(ct + 1) * TT], axis=AxX),
                waits=Wd if ct == 0 else (), inc="sV")
        p.emit("vector", lambda e: e.tensor_tensor(
            Gend[:], Gend[:], Gtmp[:], Alu.add),
            waits=[("sV", p.cur("sV"))], inc="sV")

        for k in range(NCT):
            p.emit("tensor", lambda e, k=k: e.matmul(
                PS[4][0:DS, :], WXs[:, k * DS:(k + 1) * DS],
                u_bf[:, k * TT:(k + 1) * TT],
                start=(k == 0), stop=(k == NCT - 1)),
                waits=Wu + Wwd if k == 0 else (), inc="sT")
        st = p.cur("sT")
        p.emit("scalar", lambda e: e.activation(
            Bsb[0:DS, :], PS[4][0:DS, :], Act.Identity, bias=bx[0:DS, 0:1]),
            waits=[("sT", st)], inc="sS")
        ss = p.cur("sS")
        p.emit("vector", lambda e: e.tensor_copy(Bsb_bf[0:DS, :], Bsb[0:DS, :]),
               waits=[("sS", ss)], inc="sV")
        svB = p.cur("sV")
        for s in range(DS):
            bw = [("sV", svB)] if s == 0 else [("sS", p.cur("sS"))]
            p.emit("tensor", lambda e, s=s: e.matmul(
                PS[5][:], sels[:, s * 128:(s + 1) * 128], Bsb_bf[0:DS, :],
                start=True, stop=True), waits=bw, inc="sT")
            st = p.cur("sT")
            p.emit("scalar", lambda e, s=s: e.activation(
                Bbc[:, s * TT:(s + 1) * TT], PS[5][:], Act.Identity),
                waits=[("sT", st)], inc="sS")
        Wb = p.snap()

        for ct in range(NCT):
            p.emit("vector", lambda e, ct=ct: e.tensor_tensor(
                du[:], d_bf[:, ct * TT:(ct + 1) * TT], u_bf[:, ct * TT:(ct + 1) * TT],
                Alu.mult), waits=Wb if ct == 0 else (), inc="sV")
            svdu = p.cur("sV")
            ybank = PS[6]
            for s in range(DS):
                idx = ct * DS + s
                p.emit("scalar", lambda e, ct=ct, idx=idx: e.activation(
                    abuf[:], d_bf[:, ct * TT:(ct + 1) * TT],
                    Act.Exp, scale=Asb[:, idx:idx + 1]),
                    waits=[("sV", p.cur("sV"))], inc="sS")
                p.emit("vector", lambda e, s=s: e.tensor_tensor(
                    bbuf[:], du[:], Bbc[:, s * TT:(s + 1) * TT],
                    Alu.mult), waits=[("sV", svdu)], inc="sV")
                ssa = p.cur("sS"); svb = p.cur("sV")
                p.emit("vector", lambda e, idx=idx: e.tensor_tensor_scan(
                    hbuf[:], abuf[:], bbuf[:], hcar[:, idx:idx + 1],
                    Alu.mult, Alu.add), waits=[("sS", ssa), ("sV", svb)], inc="sV")
                p.emit("vector", lambda e, idx=idx: e.tensor_scalar(
                    hcar[:, idx:idx + 1], hbuf[:, TT - 1:TT], 1.0, None,
                    Alu.mult), waits=[("sV", p.cur("sV"))], inc="sV")
                svh = p.cur("sV")
                p.emit("tensor", lambda e, s=s, ybank=ybank: e.matmul(
                    ybank[:], identbf[:], hbuf[:],
                    start=(s == 0), stop=(s == DS - 1)),
                    waits=[("sV", svh)], inc="sT")
            st = p.cur("sT")
            p.emit("vector", lambda e, ct=ct, ybank=ybank: e.scalar_tensor_tensor(
                y1_bf[:, ct * TT:(ct + 1) * TT], u_bf[:, ct * TT:(ct + 1) * TT],
                Dv[:, ct:ct + 1], ybank[:], Alu.mult, Alu.add),
                waits=[("sT", st)], inc="sV")
        Ws = p.snap()

        if DEBUG_TAPS:
            p.emit("gpsimd", lambda e, j=j: e.dma_start(out=o_B[j], in_=Bsb[0:DS, :]),
                   waits=Ws, inc="dG", incby=16)
            p.emit("gpsimd", lambda e, j=j: e.dma_start(out=o_Bbc[j], in_=Bbc),
                   waits=Ws, inc="dG", incby=16)
            p.emit("gpsimd", lambda e, j=j: e.dma_start(out=o_hc[j], in_=hcar[:]),
                   waits=Ws, inc="dG", incby=16)
            p.emit("gpsimd", lambda e, j=j: e.dma_start(out=o_u[j], in_=u_bf),
                   waits=Ws, inc="dG", incby=16)
            p.emit("gpsimd", lambda e, j=j: e.dma_start(out=o_xn[j], in_=xn_fm[:]),
                   waits=Ws, inc="dG", incby=16)
        p.emit("gpsimd", lambda e, j=j: e.dma_start(out=o_delta[j], in_=d_bf),
               waits=Ws, inc="dG", incby=16)
        p.emit("gpsimd", lambda e, j=j: e.dma_start(out=o_y1[j], in_=y1_bf),
               inc="dG", incby=16)
        p.emit("gpsimd", lambda e, j=j: e.dma_start(out=o_siluz[j], in_=sz_bf),
               inc="dG", incby=16)

    mixer_tile(-1)
    for j in range(NT):
        mixer_tile(j)

    Wg = p.snap()
    for ct in range(NCT):
        p.emit("scalar", lambda e, ct=ct: e.activation(
            aend[:, ct * DS:(ct + 1) * DS], Asb[:, ct * DS:(ct + 1) * DS],
            Act.Exp, scale=Gend[:, ct:ct + 1]), waits=Wg if ct == 0 else (), inc="sS")
    ss = p.cur("sS")
    p.emit("sync", lambda e: e.dma_start(out=o_exch[:, 0:NCT * DS], in_=aend[:]),
           waits=[("sS", ss)], inc="dS", incby=16)
    p.emit("sync", lambda e: e.dma_start(out=o_exch[:, NCT * DS:], in_=hcar[:]),
           waits=Wg, inc="dS", incby=16)
    if DEBUG_TAPS:
        p.emit("sync", lambda e: e.dma_start(out=o_exch_t[:, 0:NCT * DS], in_=aend[:]),
               waits=[("sS", p.cur("sS"))], inc="dS", incby=16)
        p.emit("sync", lambda e: e.dma_start(out=o_exch_t[:, NCT * DS:], in_=hcar[:]),
               waits=Wg, inc="dS", incby=16)

    def ffn_tile(j):
        Wm = p.snap()
        for k in range(KD):
            p.emit("gpsimd", lambda e, k=k: e.dma_start(
                out=WA[:, k * (2 * DI): k * (2 * DI) + DFF],
                in_=W1_h[k * 128:(k + 1) * 128, :]), waits=Wm, inc="dG", incby=16)
        p.emit("sync", lambda e, j=j: e.dma_start(out=xn_fm[:], in_=xn_sp[j]),
               waits=Wm, inc="dS", incby=16)
        for b in range(4):
            p.emit("sync", lambda e, j=j, b=b: e.dma_start(
                out=xacc[:, b * D:(b + 1) * D],
                in_=x_h[j * TT + b * 128: j * TT + (b + 1) * 128, :]),
                waits=Wm, inc="dS", incby=16)
        Wx_ = p.snap()

        bankfree = {}
        svq = {}
        for m in range(KFF):
            bank = PS[m % 2]
            mywait = list(Wx_)
            if m >= 2:
                mywait.append(("sS", bankfree[m - 2]))
                mywait.append(("sV", svq[m - 1]))
            for k in range(KD):
                p.emit("tensor", lambda e, m=m, k=k, bank=bank: e.matmul(
                    bank[:], WA[:, k * (2 * DI) + m * 128: k * (2 * DI) + (m + 1) * 128],
                    xn_fm[:, k * TT:(k + 1) * TT],
                    start=(k == 0), stop=(k == KD - 1)),
                    waits=mywait if k == 0 else (), inc="sT")
            st = p.cur("sT")
            qmw = [("sT", st), ("sV", svq.get(m - 1, 0))]
            if m == 0:
                qmw = qmw + list(Wx_)
            p.emit("scalar", lambda e, m=m, bank=bank: e.activation(
                qm[:], bank[:], Act.Identity, bias=biasm[:, m:m + 1], scale=0.1 * INV2PI),
                waits=qmw, inc="sS")
            p.emit("scalar", lambda e, m=m, bank=bank: e.activation(
                qh[:], bank[:], Act.Identity, bias=b1[:, m:m + 1]), inc="sS")
            bankfree[m] = p.cur("sS")
            ss = p.cur("sS")
            p.emit("vector", lambda e: e.tensor_copy(qi[:], qm[:]),
                   waits=[("sS", ss)], inc="sV")
            p.emit("vector", lambda e: e.tensor_copy(qf[:], qi[:]), inc="sV")
            p.emit("vector", lambda e: e.tensor_tensor(qw[:], qm[:], qf[:], Alu.subtract), inc="sV")
            sv = p.cur("sV")
            p.emit("scalar", lambda e: e.activation(qs[:], qw[:], Act.Sin, scale=TWOPI),
                   waits=[("sV", sv)], inc="sS")
            ss = p.cur("sS")
            p.emit("vector", lambda e, m=m: e.scalar_tensor_tensor(
                qw[:], qs[:], c1[:, m:m + 1], qh[:], Alu.mult, Alu.add),
                waits=[("sS", ss)], inc="sV")
            p.emit("vector", lambda e, m=m: e.tensor_scalar(
                actq[:, m * TT:(m + 1) * TT], qw[:], 0.0, None, Alu.max),
                waits=[("sV", p.cur("sV"))], inc="sV")
            svq[m] = p.cur("sV")
        Wq = p.snap()

        stbase = p.cur("sT")
        for k in range(KFF):
            p.emit("gpsimd", lambda e, k=k: e.dma_start(
                out=W2s[:, (k % 2) * D:(k % 2) * D + D],
                in_=W2_h[k * 128:(k + 1) * 128, :]),
                waits=list(Wq) if k < 2 else [("sT", stbase + 8 * (k - 1))],
                inc="dG", incby=16)
            dg = p.cur("dG")
            for mo in range(KD):
                p.emit("tensor", lambda e, k=k, mo=mo: e.matmul(
                    PS[mo][:], W2s[:, (k % 2) * D + mo * 128:(k % 2) * D + (mo + 1) * 128],
                    actq[:, k * TT:(k + 1) * TT],
                    start=(k == 0), stop=(k == KFF - 1)),
                    waits=([("dG", dg)] + list(Wq)) if mo == 0 else (), inc="sT")
        st = p.cur("sT")
        for mo in range(KD):
            p.emit("scalar", lambda e, mo=mo: e.activation(
                ffn_fm[:, mo * TT:(mo + 1) * TT], PS[mo][:], Act.Identity,
                bias=b2[:, mo:mo + 1]), waits=[("sT", st)] if mo == 0 else (), inc="sS")
        Wf = p.snap()

        for tb in range(4):
            for half in range(2):
                for fb in range(4):
                    mo = half * 4 + fb
                    tw = ((list(Wf) + [("sV", p.cur("sV"))])
                          if fb == 0 else ())
                    p.emit("tensor", lambda e, tb=tb, mo=mo, fb=fb: e.transpose(
                        PS[7][:, fb * 128:(fb + 1) * 128],
                        ffn_fm[:, mo * TT + tb * 128: mo * TT + (tb + 1) * 128],
                        identf[:]), waits=tw, inc="sT")
                st = p.cur("sT")
                p.emit("vector", lambda e, tb=tb, half=half: e.tensor_tensor(
                    xacc[:, tb * D + half * TT: tb * D + (half + 1) * TT],
                    xacc[:, tb * D + half * TT: tb * D + (half + 1) * TT],
                    PS[7][:], Alu.add), waits=[("sT", st)], inc="sV")
        sv = p.cur("sV")
        if DEBUG_TAPS:
            p.emit("gpsimd", lambda e, j=j: e.dma_start(out=o_actq[j], in_=actq),
                   waits=Wq, inc="dG", incby=16)
            p.emit("gpsimd", lambda e, j=j: e.dma_start(out=o_ffn[j], in_=ffn_fm),
                   waits=Wf, inc="dG", incby=16)
        p.emit("gpsimd", lambda e, j=j: e.dma_start(out=o_acc[j], in_=xacc),
               waits=[("sV", sv)], inc="dG", incby=16)

    for j in range(NT):
        ffn_tile(j)

    Wc = p.snap()
    p.emit("gpsimd", lambda e: e.collective_compute(
        "AllGather", Alu.bypass,
        replica_groups=[[0, 1, 2, 3], [4, 5, 6, 7]],
        ins=[exch_in_t.ap()],
        outs=[exch_all_t.ap()]), waits=Wc, inc="sG")
    scoll = p.cur("sG")
    for i in range(4):
        p.emit("sync", lambda e, i=i: e.dma_start(
            out=exg[:, i * 512:(i + 1) * 512], in_=exch_all[i]),
            waits=[("sG", scoll)], inc="dS", incby=16)
    Wg2 = p.snap()

    NS = NCT * DS

    def hf(fn):
        p.emit("vector", fn, waits=[("sV", p.cur("sV"))], inc="sV")

    p.emit("vector", lambda e: e.memset(Hst[:], 0.0), waits=Wg2, inc="sV")
    p.emit("vector", lambda e: e.memset(Grun[:], 0.0), inc="sV")
    hf(lambda e: e.tensor_scalar(onesm[:], pmask[:], -1.0, 1.0, Alu.mult, Alu.add))
    for i in range(3):
        hf(lambda e, i=i: e.scalar_tensor_tensor(
            tmpc[:, 0:NS], exg[:, i * 512: i * 512 + NS], pmask[:, i:i + 1],
            onesm[:, i:i + 1].broadcast_to([128, NS]), Alu.mult, Alu.add))
        hf(lambda e: e.tensor_tensor(Hst[:, 0:NS], Hst[:, 0:NS], tmpc[:, 0:NS], Alu.mult))
        hf(lambda e, i=i: e.tensor_scalar(
            tmpc[:, 0:NS], exg[:, i * 512 + NS: (i + 1) * 512], pmask[:, i:i + 1],
            None, Alu.mult))
        hf(lambda e: e.tensor_tensor(Hst[:, 0:NS], Hst[:, 0:NS], tmpc[:, 0:NS], Alu.add))
    Wp = p.snap()

    for k in range(NCT):
        p.emit("gpsimd", lambda e, k=k: e.dma_start(
            out=Wo[:, k * D:(k + 1) * D],
            in_=Wout_h[k * 128:(k + 1) * 128, :]), waits=Wc, inc="dG", incby=16)
    Wwo = p.snap()

    def corr_tile(j):
        Wm = p.snap() + Wp + Wwo
        p.emit("sync", lambda e, j=j: e.dma_start(out=d_bf_c, in_=o_delta[j]),
               waits=Wm, inc="dS", incby=16)
        p.emit("sync", lambda e, j=j: e.dma_start(out=y1_bf_c, in_=o_y1[j]),
               waits=Wm, inc="dS", incby=16)
        p.emit("sync", lambda e, j=j: e.dma_start(out=sz_bf_c, in_=o_siluz[j]),
               waits=Wm, inc="dS", incby=16)
        p.emit("sync", lambda e, j=j: e.dma_start(out=acc[:], in_=o_acc[j]),
               waits=Wm, inc="dS", incby=16)
        Wx_ = p.snap() + Wp

        for ct in range(NCT):
            p.emit("vector", lambda e, ct=ct: e.tensor_tensor_scan(
                Gbuf[:], d_bf_c[:, ct * TT:(ct + 1) * TT],
                d_bf_c[:, ct * TT:(ct + 1) * TT],
                Grun[:, ct:ct + 1], Alu.bypass, Alu.add),
                waits=Wx_ if ct == 0 else [("sS", p.cur("sS"))], inc="sV")
            p.emit("vector", lambda e, ct=ct: e.tensor_scalar(
                Grun[:, ct:ct + 1], Gbuf[:, TT - 1:TT], 1.0, None, Alu.mult),
                waits=[("sV", p.cur("sV"))], inc="sV")
            p.emit("vector", lambda e, ct=ct: e.tensor_copy(
                y1f[:], y1_bf_c[:, ct * TT:(ct + 1) * TT]), inc="sV")
            for s in range(DS):
                idx = ct * DS + s
                sv = p.cur("sV")
                p.emit("scalar", lambda e, idx=idx: e.activation(
                    Ebuf[:], Gbuf[:], Act.Exp, scale=Asb[:, idx:idx + 1]),
                    waits=[("sV", sv)], inc="sS")
                ss = p.cur("sS")
                p.emit("vector", lambda e, idx=idx: e.scalar_tensor_tensor(
                    y1f[:], Ebuf[:], Hst[:, idx:idx + 1], y1f[:], Alu.mult, Alu.add),
                    waits=[("sS", ss)], inc="sV")
            p.emit("vector", lambda e, ct=ct: e.tensor_tensor(
                y2[:, ct * TT:(ct + 1) * TT], y1f[:],
                sz_bf_c[:, ct * TT:(ct + 1) * TT], Alu.mult), inc="sV")
        Wy = p.snap()

        evs = p.cur("sS")
        for mo in range(KD):
            bank = PS[mo % 2]
            mywait = list(Wy)
            if mo >= 2:
                mywait.append(("sS", evs + (mo - 1)))
            for k in range(NCT):
                p.emit("tensor", lambda e, mo=mo, k=k, bank=bank: e.matmul(
                    bank[:], Wo[:, k * D + mo * 128: k * D + (mo + 1) * 128],
                    y2[:, k * TT:(k + 1) * TT], start=(k == 0), stop=(k == NCT - 1)),
                    waits=mywait if k == 0 else (), inc="sT")
            st = p.cur("sT")
            p.emit("scalar", lambda e, mo=mo, bank=bank: e.activation(
                attn[:, mo * TT:(mo + 1) * TT], bank[:], Act.Identity,
                bias=bout[:, mo:mo + 1]), waits=[("sT", st)], inc="sS")
        Wa = p.snap()

        for tb in range(4):
            for half in range(2):
                for fb in range(4):
                    mo = half * 4 + fb
                    tw = ((list(Wa) + [("sV", p.cur("sV"))]) if fb == 0 else ())
                    p.emit("tensor", lambda e, tb=tb, mo=mo, fb=fb: e.transpose(
                        PS[7][:, fb * 128:(fb + 1) * 128],
                        attn[:, mo * TT + tb * 128: mo * TT + (tb + 1) * 128],
                        identf[:]), waits=tw, inc="sT")
                st = p.cur("sT")
                p.emit("vector", lambda e, tb=tb, half=half: e.tensor_tensor(
                    acc[:, tb * D + half * TT: tb * D + (half + 1) * TT],
                    acc[:, tb * D + half * TT: tb * D + (half + 1) * TT],
                    PS[7][:], Alu.add), waits=[("sT", st)], inc="sV")
        sv = p.cur("sV")
        for b in range(4):
            p.emit("sync", lambda e, j=j, b=b: e.dma_start(
                out=o_out[j * TT + b * 128: j * TT + (b + 1) * 128, :],
                in_=acc[:, b * D:(b + 1) * D]),
                waits=[("sV", sv)], inc="dS", incby=16)

    for j in range(NT):
        corr_tile(j)

    fin = [("dS", p.cur("dS")), ("dG", p.cur("dG"))]
    p.emit("gpsimd", lambda e: e.memset(lnc[:, 15:16], 0.0), waits=fin)

    with es, semstack, nc.Block() as block:
        p.realize(block, sems)
    return nc



def _shared_consts(inputs):
    f = np.float32
    cc = {}
    cc["W_in"] = np.ascontiguousarray(inputs["L0_attn_W_in"], f)
    cc["W_dt"] = np.ascontiguousarray(inputs["L0_attn_W_dt"], f)
    cc["W_x16"] = np.ascontiguousarray(inputs["L0_attn_W_x"][:, DS:], f)
    cc["W1"] = np.ascontiguousarray(inputs["L0_ffn_W1"], f)
    cc["W2"] = np.ascontiguousarray(inputs["L0_ffn_W2"], f)
    b_in = inputs["L0_attn_b_in"].astype(f)
    cc["b_in_h"] = np.ascontiguousarray(b_in[:DI].reshape(NCT, 128).T)
    cc["b_in_z"] = np.ascontiguousarray(b_in[DI:].reshape(NCT, 128).T)
    cc["b_dt"] = np.ascontiguousarray(inputs["L0_attn_b_dt"].astype(f).reshape(NCT, 128).T)
    bx = np.zeros((128, 1), f)
    bx[:DS, 0] = inputs["L0_attn_b_x"].astype(f)[DS:]
    cc["b_x16"] = bx
    cc["conv_w"] = np.ascontiguousarray(
        inputs["L0_attn_conv_w"].astype(f)[:, 0, :].reshape(NCT, 128, 4)
        .transpose(1, 0, 2).reshape(128, NCT * 4))
    cc["conv_b"] = np.ascontiguousarray(inputs["L0_attn_conv_b"].astype(f).reshape(NCT, 128).T)
    cc["A_log"] = np.ascontiguousarray(
        inputs["L0_attn_A_log"].astype(f).reshape(NCT, 128, DS)
        .transpose(1, 0, 2).reshape(128, NCT * DS))
    cc["Dv"] = np.ascontiguousarray(inputs["L0_attn_D"].astype(f).reshape(NCT, 128).T)
    cc["b1"] = np.ascontiguousarray(inputs["L0_ffn_b1"].astype(f).reshape(KFF, 128).T)
    cc["theta"] = np.ascontiguousarray(
        inputs["L0_ffn_quantum_phases"].astype(f).reshape(KFF, 128).T)
    cc["wreal"] = np.ascontiguousarray(
        inputs["L0_ffn_quantum_weights_real"].astype(f).reshape(KFF, 128).T)
    cc["wimag"] = np.ascontiguousarray(
        inputs["L0_ffn_quantum_weights_imag"].astype(f).reshape(KFF, 128).T)
    cc["b2"] = np.ascontiguousarray(inputs["L0_ffn_b2"].astype(f).reshape(KD, 128).T)
    cc["identbf"] = np.eye(128, dtype=ml_dtypes.bfloat16)
    cc["identf"] = np.eye(128, dtype=f)
    sels = np.zeros((DS, DS * 128), dtype=ml_dtypes.bfloat16)
    for s in range(DS):
        sels[s, s * 128:(s + 1) * 128] = 1
    cc["sels"] = sels
    return cc


def kernel(**inputs):
    f = np.float32
    x = np.ascontiguousarray(inputs["L0_attn_x"], f).reshape(2 * 16384, D)
    cc = _shared_consts(inputs)
    cc["W_out"] = np.ascontiguousarray(inputs["L0_attn_W_out"], f)
    cc["b_out"] = np.ascontiguousarray(
        inputs["L0_attn_b_out"].astype(f).reshape(KD, 128).T)

    nc = build_fused()
    in_maps = []
    for k in range(NCORES):
        sq = k % 4
        m = dict(cc)
        m["x"] = np.ascontiguousarray(x[k * TCORE:(k + 1) * TCORE])
        m["xhalo"] = (np.ascontiguousarray(x[k * TCORE - 128:k * TCORE])
                      if sq > 0 else np.zeros((128, D), f))
        m["halo_mask"] = np.full((128, 1), 1.0 if sq > 0 else 0.0, f)
        pm = np.zeros((128, 4), f)
        pm[:, :sq] = 1.0
        m["pmask"] = pm
        in_maps.append(m)
    res = run_bass_kernel_spmd(nc, in_maps, core_ids=list(range(NCORES)))
    out = np.concatenate(
        [np.asarray(res.results[k]["o_out"], f) for k in range(NCORES)], axis=0)
    return out.reshape(2, 16384, D)


# version 08
# speedup vs baseline: 1.8498x; 1.8498x over previous
import math
import numpy as np
import ml_dtypes
import concourse.bass as bass
import concourse.mybir as mybir
from concourse.bass_utils import run_bass_kernel_spmd

F32 = mybir.dt.float32
BF16 = mybir.dt.bfloat16
I32 = mybir.dt.int32
Alu = mybir.AluOpType
Act = mybir.ActivationFunctionType
AxX = mybir.AxisListType.X

DEBUG_TAPS = False
NCORES = 8
TCORE = 4096
TT = 512
NT = TCORE // TT
D = 1024
DI = 2048
DS = 16
DFF = 4096
NCT = DI // 128
KD = D // 128
KFF = DFF // 128
UPC = TT + 4
INVPI = float(1.0 / math.pi)
PI = float(math.pi)
T_CORR_TILES = 1

ENGINES = ("sync", "scalar", "vector", "tensor", "gpsimd")


class Prog:
    def __init__(self):
        self.ops = {e: [] for e in ENGINES}
        self.cnt = {}

    def emit(self, eng, fn, waits=(), inc=None, incby=1):
        ws = [(s, v) for (s, v) in waits if v > 0]
        self.ops[eng].append((fn, ws, inc, incby))
        if inc is not None:
            self.cnt[inc] = self.cnt.get(inc, 0) + incby

    def cur(self, sem):
        return self.cnt.get(sem, 0)

    def snap(self):
        return [(s, v) for s, v in self.cnt.items()]

    def realize(self, block, sems):
        for engname in ENGINES:
            ops = self.ops[engname]
            if not ops:
                continue

            def make(engops):
                def _f(eng):
                    waited = {}
                    for fn, ws, inc, incby in engops:
                        for (sn, v) in ws:
                            if waited.get(sn, -1) < v:
                                eng.wait_ge(sems[sn], v)
                                waited[sn] = v
                        instr = fn(eng)
                        if inc is not None:
                            instr.then_inc(sems[inc], incby)
                return _f

            getattr(block, engname)(make(ops))


def build_fused():
    nc = bass.Bass("TRN2", target_bir_lowering=False)
    dt_in = lambda n, s, d=F32: nc.dram_tensor(n, s, d, kind="ExternalInput").ap()
    dt_out = lambda n, s, d=F32: nc.dram_tensor(n, s, d, kind="ExternalOutput").ap()
    if DEBUG_TAPS:
        dt_int = lambda n, s, d=F32: nc.dram_tensor(n, s, d, kind="ExternalOutput").ap()
    else:
        dt_int = lambda n, s, d=F32: nc.dram_tensor(n, s, d).ap()

    x_h = dt_in("x", [TCORE, D])
    xhalo_h = dt_in("xhalo", [128, D])
    Win_h = dt_in("W_in", [D, 2 * DI])
    Wdt_h = dt_in("W_dt", [DI, DI])
    Wx_h = dt_in("W_x16", [DI, DS])
    W1_h = dt_in("W1", [D, DFF])
    W2_h = dt_in("W2", [DFF, D])
    Wout_h = dt_in("W_out", [DI, D])
    binh_h = dt_in("b_in_h", [128, NCT])
    binz_h = dt_in("b_in_z", [128, NCT])
    bdt_h = dt_in("b_dt", [128, NCT])
    bx_h = dt_in("b_x16", [128, 1])
    cw_h = dt_in("conv_w", [128, NCT * 4])
    cb_h = dt_in("conv_b", [128, NCT])
    Alog_h = dt_in("A_log", [128, NCT * DS])
    Dv_h = dt_in("Dv", [128, NCT])
    b1_h = dt_in("b1", [128, KFF])
    th_h = dt_in("theta", [128, KFF])
    wr_h = dt_in("wreal", [128, KFF])
    wi_h = dt_in("wimag", [128, KFF])
    b2_h = dt_in("b2", [128, KD])
    bout_h = dt_in("b_out", [128, KD])
    identbf_h = dt_in("identbf", [128, 128], BF16)
    sels_h = dt_in("sels", [DS, DS * 128], BF16)
    hmask_h = dt_in("halo_mask", [128, 1])
    pmask_h = dt_in("pmask", [128, 4])

    o_out = dt_out("o_out", [TCORE, D])

    wbf_in = dt_int("wbf_in", [8, 128, KD * TT], BF16)
    wbf_w1 = dt_int("wbf_w1", [8, 128, KD * TT], BF16)
    wbf_w2 = dt_int("wbf_w2", [KD, 128, KFF * 128], BF16)
    wbf_wo = dt_int("wbf_wo", [KD, 128, NCT * 128], BF16)
    sp_delta = dt_int("sp_delta", [NT, 128, NCT * TT], BF16)
    sp_u = dt_int("sp_u", [NT, 128, NCT * TT], BF16)
    sp_sz = dt_int("sp_sz", [NT, 128, NCT * TT], BF16)
    sp_B = dt_int("sp_B", [NT, 16, TT])
    sp_y1 = dt_int("sp_y1", [T_CORR_TILES, 128, NCT * TT], BF16)
    sp_acc = dt_int("sp_acc", [T_CORR_TILES, 128, 4 * D])
    xn_sp = dt_int("xn_sp", [NT, 128, KD * TT], BF16)
    if DEBUG_TAPS:
        tap_pre = dt_out("tap_pre", [128, NCT * UPC], BF16)
        tap_post = dt_out("tap_post", [128, NCT * UPC], BF16)
    exch_in_t = nc.dram_tensor("exch_in", [128, 2 * NCT * DS], F32)
    exch_all_t = nc.dram_tensor("exch_all", [4, 128, 2 * NCT * DS], F32)
    exch_all = exch_all_t.ap()
    exch_in = exch_in_t.ap()

    from contextlib import ExitStack
    es = ExitStack()
    S = lambda n, sh, d=F32: es.enter_context(nc.sbuf_tensor("sb_" + n, sh, d))

    binh = S("binh", [128, NCT]); binz = S("binz", [128, NCT])
    bdt = S("bdt", [128, NCT]); bx = S("bx", [128, 1])
    cw = S("cw", [128, NCT * 4]); cb = S("cb", [128, NCT])
    Asb = S("Asb", [128, NCT * DS]); Dv = S("Dv", [128, NCT])
    b1 = S("b1", [128, KFF]); th = S("th", [128, KFF])
    wr = S("wr", [128, KFF]); wi = S("wi", [128, KFF])
    b2 = S("b2", [128, KD]); bout = S("bout", [128, KD])
    identbf = S("identbf", [128, 128], BF16)
    sels = S("sels", [DS, DS * 128], BF16)
    hmask = S("hmask", [128, 1]); pmask = S("pmask", [128, 4])
    c1s = S("c1s", [128, KFF])
    bsin = S("bsin", [128, KFF])
    hcar = S("hcar", [128, NCT * DS])
    Gend = S("Gend", [128, NCT]); Gtmp = S("Gtmp", [128, NCT])
    aend = S("aend", [128, NCT * DS])
    tA = S("tA", [128, KFF]); tB = S("tB", [128, KFF])
    tC = S("tC", [128, KFF]); tI = S("tI", [128, KFF], I32)
    lnc = S("lnc", [128, 16])
    carry = S("carry", [128, NCT * 3])

    AR = S("AR", [128, 101376], BF16)

    def A_(c0, c1, dt=None):
        v = AR[:, c0:c1]
        return v.bitcast(dt) if dt else v

    p1_upre = [A_(0, 8256), A_(8256, 16512)]
    p1_wdt = A_(16512, 16512 + 32768)
    P1W = 49280
    p1_d = A_(P1W, P1W + 8192)
    p1_sz = A_(P1W + 8192, P1W + 16384)
    p1_xtok = [A_(P1W + 16384, P1W + 20480), A_(P1W + 20480, P1W + 24576)]
    p1_xntok = A_(P1W + 24576, P1W + 28672)
    p1_xnfm = [A_(P1W + 28672, P1W + 32768), A_(P1W + 32768, P1W + 36864)]
    p1_win = [A_(P1W + 36864, P1W + 40960), A_(P1W + 40960, P1W + 45056)]
    p1_sqs = A_(P1W + 45056, P1W + 47104, F32)
    p1_B = A_(P1W + 47104, P1W + 48128, F32)
    p1_cb = A_(P1W + 48128, P1W + 49152, F32)
    p1_cb2 = A_(98944, 99968, F32)
    p1_wx = A_(P1W + 49152, P1W + 49664)
    p2_d = A_(0, 8192)
    p2_u = A_(8192, 16384)
    p2_bbc = A_(16384, 24576)
    p2_xn = A_(24576, 28672)
    p2_actq = A_(28672, 45056)
    p2_y1 = A_(45056, 53248)
    p2_sz = A_(53248, 61440)
    p2_acc = A_(61440, 69632, F32)
    p2_w1 = [A_(69632, 73728), A_(73728, 77824)]
    p2_w2 = [A_(77824, 81920), A_(81920, 86016)]
    p2_wo = [A_(86016, 88064), A_(88064, 90112)]
    p2_a = [A_(90112, 90624), A_(90624, 91136), A_(98304, 98816),
            A_(98816, 99328), A_(99328, 99840)]
    p2_b = [A_(91136, 91648), A_(91648, 92160)]
    p2_h = [A_(92160, 92672), A_(92672, 93184), A_(99840, 100352),
            A_(100352, 100864)]
    p2_du = A_(93184, 93696)
    p2_qh = A_(93696, 94720, F32)
    p2_qs = A_(94720, 95744, F32)
    p2_B2 = A_(95744, 96768, F32)
    p2_at = [A_(96768, 97280), A_(97280, 97792)]
    p2_bst = A_(97792, 98304)
    p3_G = A_(28672, 29696, F32)
    p3_y1f = A_(29696, 30720, F32)
    p3_E = A_(30720, 31744, F32)
    p3_Gr = A_(31744, 31776, F32)
    p3_exg = A_(31776, 35872, F32)
    p3_H = A_(35872, 36384, F32)
    p3_tmp = A_(36384, 36896, F32)
    p3_on = A_(36896, 36904, F32)

    sems = {}
    semstack = ExitStack()
    for name in ("dS", "dG", "sT", "sS", "sV", "sY", "sG"):
        sems[name] = semstack.enter_context(nc.semaphore(name))
    p = Prog()

    PS = [es.enter_context(nc.psum_tensor(f"ps{i}", [128, TT], F32)) for i in range(8)]

    def ld(dst, src):
        p.emit("sync", lambda e, d=dst, s=src: e.dma_start(out=d, in_=s),
               inc="dS", incby=16)

    ld(binh[:], binh_h[:]); ld(binz[:], binz_h[:]); ld(bdt[:], bdt_h[:])
    ld(bx[:], bx_h[:]); ld(cw[:], cw_h[:]); ld(cb[:], cb_h[:])
    ld(Asb[:], Alog_h[:]); ld(Dv[:], Dv_h[:]); ld(b1[:], b1_h[:])
    ld(th[:], th_h[:]); ld(wr[:], wr_h[:]); ld(wi[:], wi_h[:])
    ld(b2[:], b2_h[:]); ld(bout[:], bout_h[:])
    ld(identbf[:], identbf_h[:]); ld(sels[:], sels_h[:])
    ld(hmask[:], hmask_h[:]); ld(pmask[:], pmask_h[:])
    W = p.snap()

    for kk in range(8):
        p.emit("gpsimd", lambda e, kk=kk: e.dma_start(
            out=wbf_in[:, :, kk * TT:(kk + 1) * TT],
            in_=Win_h[kk * 128:(kk + 1) * 128, :].rearrange(
                "p (g c) -> g p c", c=TT)), inc="dG", incby=16)
    pre_win = p.cur("dG")
    for kk in range(NCT):
        p.emit("gpsimd", lambda e, kk=kk: e.dma_start(
            out=p1_wdt[:, kk * DI:(kk + 1) * DI],
            in_=Wdt_h[kk * 128:(kk + 1) * 128, :]), inc="dG", incby=16)
    pre_wdt = p.cur("dG")
    for kk in range(NCT):
        p.emit("gpsimd", lambda e, kk=kk: e.dma_start(
            out=p1_wx[:, kk * DS:(kk + 1) * DS],
            in_=Wx_h[kk * 128:(kk + 1) * 128, :]), inc="dG", incby=16)
    pre_wx = p.cur("dG")
    for kk in range(8):
        p.emit("gpsimd", lambda e, kk=kk: e.dma_start(
            out=wbf_w1[:, :, kk * TT:(kk + 1) * TT],
            in_=W1_h[kk * 128:(kk + 1) * 128, :].rearrange(
                "p (g c) -> g p c", c=TT)), inc="dG", incby=16)
    pre_w1 = p.cur("dG")
    for kk in range(KFF):
        p.emit("gpsimd", lambda e, kk=kk: e.dma_start(
            out=wbf_w2[:, :, kk * 128:(kk + 1) * 128],
            in_=W2_h[kk * 128:(kk + 1) * 128, :].rearrange(
                "p (mo c) -> mo p c", c=128)), inc="dG", incby=16)
    pre_w2 = p.cur("dG")
    for kk in range(NCT):
        p.emit("gpsimd", lambda e, kk=kk: e.dma_start(
            out=wbf_wo[:, :, kk * 128:(kk + 1) * 128],
            in_=Wout_h[kk * 128:(kk + 1) * 128, :].rearrange(
                "p (mo c) -> mo p c", c=128)), inc="dG", incby=16)
    pre_wo = p.cur("dG")

    def vf(fn):
        p.emit("vector", fn, waits=[("sV", p.cur("sV")), ("sS", p.cur("sS"))], inc="sV")

    p.emit("scalar", lambda e: e.activation(Asb[:], Asb[:], Act.Exp), waits=W, inc="sS")
    vf(lambda e: e.tensor_scalar(Asb[:], Asb[:], -1.0, None, Alu.mult))
    vf(lambda e: e.tensor_scalar(tA[:], th[:], INVPI, None, Alu.mult))
    vf(lambda e: e.tensor_scalar(tI[:], tA[:], 1.0, None, Alu.mult))
    vf(lambda e: e.tensor_scalar(tC[:], tI[:], 1.0, None, Alu.mult))
    vf(lambda e: e.scalar_tensor_tensor(tA[:], tC[:], -PI, th[:], Alu.mult, Alu.add))
    vf(lambda e: e.tensor_scalar(tB[:], b1[:], 0.1, None, Alu.mult))
    vf(lambda e: e.tensor_tensor(bsin[:], tA[:], tB[:], Alu.add))
    sv = p.cur("sV")
    p.emit("scalar", lambda e: e.activation(tC[:], tA[:], Act.Sin),
           waits=[("sV", sv)], inc="sS")
    p.emit("scalar", lambda e: e.activation(tB[:], tA[:], Act.Sin, scale=0.5), inc="sS")
    ss = p.cur("sS")
    p.emit("vector", lambda e: e.tensor_tensor(tB[:], tB[:], tB[:], Alu.mult),
           waits=[("sS", ss), ("sV", p.cur("sV"))], inc="sV")
    vf(lambda e: e.tensor_scalar(tB[:], tB[:], -2.0, 1.0, Alu.mult, Alu.add))
    vf(lambda e: e.tensor_tensor(tB[:], tB[:], wr[:], Alu.mult))
    vf(lambda e: e.tensor_tensor(tC[:], tC[:], wi[:], Alu.mult))
    vf(lambda e: e.tensor_tensor(tB[:], tB[:], tC[:], Alu.add))
    vf(lambda e: e.tensor_tensor(tB[:], tB[:], wi[:], Alu.mult))
    vf(lambda e: e.tensor_scalar(c1s[:], tB[:], 0.1, None, Alu.mult))
    vf(lambda e: e.memset(hcar[:], 0.0))
    vf(lambda e: e.memset(Gend[:], 0.0))
    vf(lambda e: e.memset(p1_upre[0][:, :], 0.0))
    vf(lambda e: e.memset(p1_upre[1][:, :], 0.0))
    vf(lambda e: e.memset(carry[:], 0.0))
    prep_sv = p.cur("sV")
    prep_ss = p.cur("sS")

    lnuse = {}; lnuse_v = {}
    xnfree = {}
    dspill = {}; szspill = {}; uspill = {}; bspill = {}; xnspill = {}
    upre_free = {}

    def front_tile(j):
        tt = 128 if j < 0 else TT
        ntb = tt // 128
        par = (j + 2) % 2

        xw = [("dG", pre_wx), ("sS", lnuse.get(j - 2, 0)),
              ("sV", lnuse_v.get(j - 2, 0))]
        if j < 0:
            p.emit("gpsimd", lambda e: e.dma_start(
                out=p1_xtok[par][:, 0:D], in_=xhalo_h[:]), waits=xw, inc="dG", incby=16)
        else:
            for b in range(4):
                p.emit("gpsimd", lambda e, j=j, b=b: e.dma_start(
                    out=p1_xtok[par][:, b * D:(b + 1) * D],
                    in_=x_h[j * TT + b * 128: j * TT + (b + 1) * 128, :]),
                    waits=xw, inc="dG", incby=16)
        xd = p.cur("dG")

        lw = [("dG", xd), ("sV", prep_sv), ("sS", prep_ss)]
        for b in range(ntb):
            xb = p1_xtok[par][:, b * D:(b + 1) * D]
            p.emit("scalar", lambda e, xb=xb, b=b: e.activation(
                p1_sqs[:], xb, Act.Square, accum_out=lnc[:, b + 4:b + 5]),
                waits=lw, inc="sS")
            p.emit("vector", lambda e, xb=xb, b=b: e.reduce_sum(
                lnc[:, b:b + 1], xb, axis=AxX), waits=lw, inc="sV")

        def lf(eng, fn):
            p.emit(eng, fn, waits=[("sV", p.cur("sV")), ("sS", p.cur("sS"))],
                   inc="sV" if eng == "vector" else "sS")

        nb = ntb
        lf("vector", lambda e: e.tensor_scalar(
            lnc[:, 0:nb], lnc[:, 0:nb], 1.0 / D, None, Alu.mult))
        lf("vector", lambda e: e.tensor_scalar(
            lnc[:, 4:4 + nb], lnc[:, 4:4 + nb], 1.0 / D, None, Alu.mult))
        lf("vector", lambda e: e.tensor_tensor(
            lnc[:, 8:8 + nb], lnc[:, 0:nb], lnc[:, 0:nb], Alu.mult))
        lf("vector", lambda e: e.tensor_tensor(
            lnc[:, 4:4 + nb], lnc[:, 4:4 + nb], lnc[:, 8:8 + nb], Alu.subtract))
        lf("vector", lambda e: e.tensor_scalar(
            lnc[:, 4:4 + nb], lnc[:, 4:4 + nb], 1.0, 1e-5, Alu.mult, Alu.add))
        lf("scalar", lambda e: e.activation(lnc[:, 4:4 + nb], lnc[:, 4:4 + nb], Act.Ln))
        lf("scalar", lambda e: e.activation(
            lnc[:, 4:4 + nb], lnc[:, 4:4 + nb], Act.Exp, scale=-0.5))
        lf("vector", lambda e: e.tensor_tensor(
            lnc[:, 8:8 + nb], lnc[:, 0:nb], lnc[:, 4:4 + nb], Alu.mult))
        lf("vector", lambda e: e.tensor_scalar(
            lnc[:, 8:8 + nb], lnc[:, 8:8 + nb], -1.0, None, Alu.mult))
        lnuse_v[j] = p.cur("sV")
        sv = p.cur("sV")
        for b in range(ntb):
            p.emit("scalar", lambda e, b=b: e.activation(
                p1_xntok[:, b * D:(b + 1) * D], p1_xtok[par][:, b * D:(b + 1) * D],
                Act.Identity, bias=lnc[:, b + 8:b + 9], scale=lnc[:, b + 4:b + 5]),
                waits=[("sV", sv)], inc="sS")
        lnuse[j] = p.cur("sS")

        ps7bf = PS[7][:].bitcast(BF16)
        tw0 = [("sS", lnuse[j]), ("dG", xnspill.get(j - 2, 0)),
               ("sV", p.cur("sV"))]
        for fb in range(KD):
            for tb in range(ntb):
                tw = (tw0 if fb == 0 else [("sV", p.cur("sV"))]) if tb == 0 else ()
                p.emit("tensor", lambda e, fb=fb, tb=tb, ps7bf=ps7bf: e.transpose(
                    ps7bf[:, tb * 128:(tb + 1) * 128],
                    p1_xntok[:, tb * D + fb * 128: tb * D + (fb + 1) * 128],
                    identbf[:]), waits=tw, inc="sT")
            st = p.cur("sT")
            p.emit("vector", lambda e, fb=fb, ntb=ntb, ps7bf=ps7bf: e.tensor_copy(
                p1_xnfm[par][:, fb * TT: fb * TT + ntb * 128], ps7bf[:, 0:ntb * 128]),
                waits=[("sT", st)], inc="sV")
        xnfree[j] = p.cur("sT")
        xnready = p.cur("sV")
        if j >= 0:
            p.emit("gpsimd", lambda e, j=j, par=par: e.dma_start(
                out=xn_sp[j], in_=p1_xnfm[par][:]),
                waits=[("sV", xnready)], inc="dG", incby=16)
            xnspill[j] = p.cur("dG")

        ngrp = 4 if j < 0 else 8
        evfree = {}
        for g in range(ngrp):
            gw = [("dG", pre_win), ("sT", upre_free.get((j, g - 2), 0))]
            p.emit("gpsimd", lambda e, g=g: e.dma_start(
                out=p1_win[g % 2][:], in_=wbf_in[g]), waits=gw, inc="dG", incby=16)
            wd = p.cur("dG")
            for mi in range(4):
                m = g * 4 + mi
                bank = PS[mi]
                mw = [("dG", wd), ("sV", xnready),
                      ("sS", evfree.get((g - 1, mi), 0))]
                if m == 0:
                    mw.append(("sT", upre_free.get((j - 2, "done"), 0)))
                    mw.append(("dG", uspill.get(j - 2, 0)))
                    mw.append(("sV", upre_free.get((j - 2, "conv"), 0)))
                for k in range(KD):
                    p.emit("tensor", lambda e, m=m, k=k, bank=bank, ntb=ntb, g=g, mi=mi:
                        e.matmul(
                            bank[:, 0:ntb * 128],
                            p1_win[g % 2][:, k * TT + mi * 128: k * TT + (mi + 1) * 128],
                            p1_xnfm[par][:, k * TT: k * TT + ntb * 128],
                            start=(k == 0), stop=(k == KD - 1)),
                        waits=mw if k == 0 else (), inc="sT")
                st = p.cur("sT")
                if m < NCT:
                    p.emit("scalar", lambda e, m=m, bank=bank, ntb=ntb, par=par:
                        e.activation(
                            p1_upre[par][:, m * UPC + 3: m * UPC + 3 + ntb * 128],
                            bank[:, 0:ntb * 128], Act.Identity, bias=binh[:, m:m + 1]),
                        waits=[("sT", st)], inc="sS")
                else:
                    mz = m - NCT
                    szw = [("sT", st), ("dG", szspill.get(j - 1, 0))]
                    p.emit("scalar", lambda e, mz=mz, bank=bank, ntb=ntb: e.activation(
                        p1_sz[:, mz * TT: mz * TT + ntb * 128],
                        bank[:, 0:ntb * 128], Act.Silu, bias=binz[:, mz:mz + 1]),
                        waits=szw, inc="sS")
                evfree[(g, mi)] = p.cur("sS")
            upre_free[(j, g)] = p.cur("sT")
        upre_free[(j, "xz")] = p.cur("sT")
        xz_done = p.cur("sS")

        dbg_blk = []
        if DEBUG_TAPS and j == 0:
            p.emit("gpsimd", lambda e, par=par: e.dma_start(
                out=tap_pre[:], in_=p1_upre[par][:, :]),
                waits=[("sS", xz_done), ("sV", p.cur("sV"))], inc="dG", incby=16)
            dbg_blk = [("dG", p.cur("dG"))]
        if j < 0:
            for ct in range(NCT):
                p.emit("vector", lambda e, ct=ct, par=par: e.tensor_scalar(
                    carry[:, ct * 3:(ct + 1) * 3],
                    p1_upre[par][:, ct * UPC + 128: ct * UPC + 131],
                    hmask[:, 0:1], None, Alu.mult),
                    waits=[("sS", xz_done)] if ct == 0 else (), inc="sV")
            upre_free[(j, "conv")] = p.cur("sV")
            upre_free[(j, "done")] = p.cur("sT")
            return

        silu_ss = {}
        for ct in range(NCT):
            base = ct * UPC
            cwv = [("sS", evfree[(ct // 4, ct % 4)]), ("sV", p.cur("sV")),
                   ("sS", silu_ss.get(ct - 2, 0))] + dbg_blk
            p.emit("vector", lambda e, base=base, ct=ct, par=par: e.tensor_scalar(
                p1_upre[par][:, base:base + 3], carry[:, ct * 3:(ct + 1) * 3],
                1.0, None, Alu.mult), waits=cwv, inc="sV")
            cbv = p1_cb if ct % 2 == 0 else p1_cb2
            p.emit("vector", lambda e, base=base, ct=ct, par=par, cbv=cbv:
                e.tensor_scalar(
                    cbv[:], p1_upre[par][:, base:base + TT], cw[:, ct * 4:ct * 4 + 1],
                    None, Alu.mult), inc="sV")
            for k in range(1, 4):
                p.emit("vector", lambda e, base=base, ct=ct, k=k, par=par, cbv=cbv:
                    e.scalar_tensor_tensor(
                        cbv[:], p1_upre[par][:, base + k:base + k + TT],
                        cw[:, ct * 4 + k:ct * 4 + k + 1], cbv[:],
                        Alu.mult, Alu.add), inc="sV")
            p.emit("vector", lambda e, base=base, ct=ct, par=par: e.tensor_scalar(
                carry[:, ct * 3:(ct + 1) * 3],
                p1_upre[par][:, base + TT:base + TT + 3],
                1.0, None, Alu.mult), inc="sV")
            sv = p.cur("sV")
            p.emit("scalar", lambda e, ct=ct, base=base, par=par, cbv=cbv: e.activation(
                p1_upre[par][:, base + 3:base + 3 + TT], cbv[:], Act.Silu,
                bias=cb[:, ct:ct + 1]), waits=[("sV", sv)], inc="sS")
            silu_ss[ct] = p.cur("sS")
        silu_done = p.cur("sS")
        upre_free[(j, "conv")] = p.cur("sV")
        if DEBUG_TAPS and j == 0:
            p.emit("gpsimd", lambda e, par=par: e.dma_start(
                out=tap_post[:], in_=p1_upre[par][:, :]),
                waits=[("sS", silu_done)], inc="dG", incby=16)

        evs = {}
        for g in range(4):
            for mi in range(4):
                m = g * 4 + mi
                bank = PS[mi]
                mw = [("dG", pre_wdt), ("sS", evs.get((g - 1, mi), 0))]
                if m == 0:
                    mw.append(("dG", dspill.get(j - 1, 0)))
                for k in range(NCT):
                    kw = (mw if k == 0 else []) + [("sS", silu_ss[k])]
                    p.emit("tensor", lambda e, m=m, k=k, bank=bank, par=par: e.matmul(
                        bank[:], p1_wdt[:, k * DI + m * 128: k * DI + (m + 1) * 128],
                        p1_upre[par][:, k * UPC + 3: k * UPC + 3 + TT],
                        start=(k == 0), stop=(k == NCT - 1)),
                        waits=kw, inc="sT")
                st = p.cur("sT")
                p.emit("scalar", lambda e, bank=bank, m=m: e.activation(
                    p1_cb[:], bank[:], Act.Exp, bias=bdt[:, m:m + 1]),
                    waits=[("sT", st)], inc="sS")
                p.emit("scalar", lambda e, m=m: e.activation(
                    p1_d[:, m * TT:(m + 1) * TT], p1_cb[:], Act.Ln, bias=1.0), inc="sS")
                evs[(g, mi)] = p.cur("sS")
        d_done = p.cur("sS")

        for ct in range(NCT):
            p.emit("vector", lambda e, ct=ct: e.reduce_sum(
                Gtmp[:, ct:ct + 1], p1_d[:, ct * TT:(ct + 1) * TT], axis=AxX),
                waits=[("sS", d_done)] if ct == 0 else (), inc="sV")
        p.emit("vector", lambda e: e.tensor_tensor(
            Gend[:], Gend[:], Gtmp[:], Alu.add),
            waits=[("sV", p.cur("sV"))], inc="sV")

        bw = [("dG", pre_wx)]
        for k in range(NCT):
            kw = (bw if k == 0 else []) + [("sS", silu_ss[k])]
            p.emit("tensor", lambda e, k=k, par=par: e.matmul(
                PS[4][0:DS, :], p1_wx[:, k * DS:(k + 1) * DS],
                p1_upre[par][:, k * UPC + 3: k * UPC + 3 + TT],
                start=(k == 0), stop=(k == NCT - 1)),
                waits=kw, inc="sT")
        st = p.cur("sT")
        p.emit("scalar", lambda e: e.activation(
            p1_B[0:DS, 0:TT], PS[4][0:DS, :], Act.Identity, bias=bx[0:DS, 0:1]),
            waits=[("sT", st), ("dG", bspill.get(j - 1, 0))], inc="sS")
        b_done = p.cur("sS")
        upre_free[(j, "done")] = p.cur("sT")

        p.emit("gpsimd", lambda e, j=j: e.dma_start(out=sp_delta[j], in_=p1_d[:, :]),
               waits=[("sS", d_done)], inc="dG", incby=16)
        dspill[j] = p.cur("dG")
        p.emit("gpsimd", lambda e, j=j, par=par: e.dma_start(
            out=sp_u[j].rearrange("p (c w) -> p c w", w=TT),
            in_=p1_upre[par].rearrange("p (c w) -> p c w", w=UPC)[:, :, 3:3 + TT]),
            waits=[("sS", silu_done)], inc="dG", incby=16)
        uspill[j] = p.cur("dG")
        p.emit("gpsimd", lambda e, j=j: e.dma_start(out=sp_sz[j], in_=p1_sz[:, :]),
               waits=[("sS", xz_done)], inc="dG", incby=16)
        szspill[j] = p.cur("dG")
        p.emit("gpsimd", lambda e, j=j: e.dma_start(out=sp_B[j], in_=p1_B[0:DS, 0:TT]),
               waits=[("sS", b_done)], inc="dG", incby=16)
        bspill[j] = p.cur("dG")

    front_tile(-1)
    for j in range(NT):
        front_tile(j)

    Wg = p.snap()
    for ct in range(NCT):
        p.emit("scalar", lambda e, ct=ct: e.activation(
            aend[:, ct * DS:(ct + 1) * DS], Asb[:, ct * DS:(ct + 1) * DS],
            Act.Exp, scale=Gend[:, ct:ct + 1]), waits=Wg if ct == 0 else (), inc="sS")
    aend_done = p.cur("sS")

    P1done = p.snap()
    p.emit("vector", lambda e: e.memset(lnc[:, 15:16], 0.0), waits=P1done, inc="sV")
    p.emit("scalar", lambda e: e.copy(lnc[:, 14:15], lnc[:, 15:16]),
           waits=P1done, inc="sS")
    p.emit("tensor", lambda e: e.matmul(PS[3][:, 0:128], identbf[:],
                                        identbf[:], start=True, stop=True),
           waits=P1done, inc="sT")
    p.emit("gpsimd", lambda e: e.memset(lnc[:, 13:14], 0.0), waits=P1done, inc="dG",
           incby=1)
    barrier_fix = p.cur("dG")

    dld = {}; uld = {}; szld = {}; xnld = {}; bld = {}; xld = {}
    scan_done = {}; y1_done = {}; actq_done = {}; ffn_used_xn = {}
    acc_done = {}; out_done = {}; y2_done = {}
    NAB = 5
    NHB = 4
    apar = {i: 0 for i in range(NAB)}
    hpar = {i: 0 for i in range(NHB)}
    ybank_free = {0: 0, 1: 0}
    hid_bank = {0: 0, 1: 0}
    w1_free = {0: 0, 1: 0}
    y1sp = {"dg": 0}
    accsp = {"dg": 0}

    def reloads(j):
        if j - 1 in scan_done:
            dw = [("sV", scan_done[j - 1]), ("sS", p.cur("sS"))]
        else:
            dw = list(P1done) + [("sV", p.cur("sV")), ("sS", p.cur("sS")),
                                 ("sT", p.cur("sT"))]
        p.emit("sync", lambda e, j=j: e.dma_start(out=p2_d, in_=sp_delta[j]),
               waits=dw + [("dG", dspill[j])], inc="dS", incby=16)
        dld[j] = p.cur("dS")
        p.emit("sync", lambda e, j=j: e.dma_start(out=p2_u, in_=sp_u[j]),
               waits=dw + [("dG", uspill[j]), ("sV", y1_done.get(j - 1, 0))],
               inc="dS", incby=16)
        uld[j] = p.cur("dS")
        p.emit("sync", lambda e, j=j: e.dma_start(out=p2_B2[0:DS, 0:TT], in_=sp_B[j]),
               waits=[("dG", bspill[j]), ("sT", p.cur("sT")), ("sV", p.cur("sV"))],
               inc="dS", incby=16)
        bld[j] = p.cur("dS")
        p.emit("sync", lambda e, j=j: e.dma_start(out=p2_xn, in_=xn_sp[j]),
               waits=[("dG", xnspill[j]), ("sT", ffn_used_xn.get(j - 1, 0))],
               inc="dS", incby=16)
        xnld[j] = p.cur("dS")
        p.emit("sync", lambda e, j=j: e.dma_start(out=p2_sz, in_=sp_sz[j]),
               waits=[("dG", szspill[j]), ("sV", y2_done.get(j - 1, 0)),
                      ("sT", p.cur("sT"))],
               inc="dS", incby=16)
        szld[j] = p.cur("dS")
        xw = [("dS", out_done.get(j - 1, 0)), ("sV", acc_done.get(j - 1, 0)),
              ("dG", accsp["dg"])]
        for b in range(4):
            p.emit("sync", lambda e, j=j, b=b: e.dma_start(
                out=p2_acc[:, b * D:(b + 1) * D],
                in_=x_h[j * TT + b * 128: j * TT + (b + 1) * 128, :]),
                waits=xw, inc="dS", incby=16)
        xld[j] = p.cur("dS")

    bst_free = {"t": 0}

    def bbc_compute(j):
        bw0 = [("dS", bld[j]), ("sV", scan_done.get(j - 1, prep_sv)),
               ("sS", p.cur("sS")), ("sT", bst_free["t"])]
        p.emit("vector", lambda e: e.tensor_copy(
            p2_bst[0:DS, 0:TT], p2_B2[0:DS, 0:TT]), waits=bw0, inc="sV")
        svb = p.cur("sV")
        for s in range(DS):
            bw = [("sV", svb)] if s == 0 else [("sS", p.cur("sS"))]
            p.emit("tensor", lambda e, s=s: e.matmul(
                PS[3][:], sels[:, s * 128:(s + 1) * 128], p2_bst[0:DS, 0:TT],
                start=True, stop=True), waits=bw, inc="sT")
            st = p.cur("sT")
            p.emit("scalar", lambda e, s=s: e.activation(
                p2_bbc[:, s * TT:(s + 1) * TT], PS[3][:], Act.Identity),
                waits=[("sT", st)], inc="sS")
        bst_free["t"] = p.cur("sT")
        return p.cur("sS")

    def ffn_piece(j, ct):
        for mi in range(2):
            m = ct * 2 + mi
            g, gm = m // 4, m % 4
            if gm == 0:
                gw = [("dG", pre_w1), ("sT", w1_free[g % 2])]
                p.emit("gpsimd", lambda e, g=g: e.dma_start(
                    out=p2_w1[g % 2][:], in_=wbf_w1[g]), waits=gw, inc="dG", incby=16)
            wd = p.cur("dG")
            bank = PS[m % 2]
            mw = [("dG", wd), ("dS", xnld[j]), ("sS", hid_bank[m % 2]),
                  ("sV", actq_done.get(j - 1, 0) if m <= 1 else 0)]
            for k in range(KD):
                p.emit("tensor", lambda e, m=m, k=k, bank=bank, g=g, gm=gm: e.matmul(
                    bank[:], p2_w1[g % 2][:, k * TT + gm * 128: k * TT + (gm + 1) * 128],
                    p2_xn[:, k * TT:(k + 1) * TT],
                    start=(k == 0), stop=(k == KD - 1)),
                    waits=mw if k == 0 else (), inc="sT")
            st = p.cur("sT")
            w1_free[g % 2] = st
            p.emit("scalar", lambda e, m=m, bank=bank: e.activation(
                p2_qh[:], bank[:], Act.Identity, bias=b1[:, m:m + 1]),
                waits=[("sT", st), ("sV", p.cur("sV"))], inc="sS")
            p.emit("scalar", lambda e, m=m, bank=bank: e.activation(
                p2_qs[:], bank[:], Act.Sin, scale=0.1, bias=bsin[:, m:m + 1]), inc="sS")
            hid_bank[m % 2] = p.cur("sS")
            ss = p.cur("sS")
            sttw = [("sS", ss)]
            if m == 0:
                sttw.append(("sT", w2_done.get(j - 1, 0)))
            p.emit("vector", lambda e, m=m: e.scalar_tensor_tensor(
                p2_qh[:], p2_qs[:], c1s[:, m:m + 1], p2_qh[:], Alu.mult, Alu.add),
                waits=sttw, inc="sV")
            p.emit("vector", lambda e, m=m: e.tensor_scalar(
                p2_actq[:, m * TT:(m + 1) * TT], p2_qh[:], 0.0, None, Alu.max),
                waits=[("sV", p.cur("sV"))], inc="sV")
        if ct == 15:
            actq_done[j] = p.cur("sV")
            ffn_used_xn[j] = p.cur("sT")

    def scan_ct(j, ct, bbc_ready):
        duw = [("dS", dld[j]), ("dS", uld[j]), ("sV", p.cur("sV"))]
        p.emit("vector", lambda e, ct=ct: e.tensor_tensor(
            p2_du[:], p2_d[:, ct * TT:(ct + 1) * TT], p2_u[:, ct * TT:(ct + 1) * TT],
            Alu.mult), waits=duw, inc="sV")
        svdu = p.cur("sV")
        ybank = PS[5 + (ct % 2)]
        for s in range(DS):
            idx = ct * DS + s
            pb = (ct * DS + s) % NAB
            hb = (ct * DS + s) % NHB
            aw = [("sV", apar[pb]), ("dS", dld[j])]
            if s == 0:
                aw.append(("sS", bbc_ready))
            p.emit("scalar", lambda e, ct=ct, idx=idx, pb=pb: e.activation(
                p2_a[pb][:], p2_d[:, ct * TT:(ct + 1) * TT],
                Act.Exp, scale=Asb[:, idx:idx + 1]), waits=aw, inc="sS")
            ssa = p.cur("sS")
            p.emit("vector", lambda e, s=s, pb=pb: e.tensor_tensor(
                p2_b[s % 2][:], p2_du[:], p2_bbc[:, s * TT:(s + 1) * TT],
                Alu.mult), waits=[("sV", svdu), ("sS", bbc_ready)], inc="sV")
            scw = [("sS", ssa), ("sY", hpar[hb])]
            p.emit("vector", lambda e, idx=idx, pb=pb, hb=hb, s=s: e.tensor_tensor_scan(
                p2_h[hb][:], p2_a[pb][:], p2_b[s % 2][:], hcar[:, idx:idx + 1],
                Alu.mult, Alu.add), waits=scw, inc="sV")
            apar[pb] = p.cur("sV")
            p.emit("vector", lambda e, idx=idx, hb=hb: e.tensor_scalar(
                hcar[:, idx:idx + 1], p2_h[hb][:, TT - 1:TT], 1.0, None,
                Alu.mult), waits=[("sV", p.cur("sV"))], inc="sV")
            svh = p.cur("sV")
            p.emit("tensor", lambda e, s=s, hb=hb, ybank=ybank: e.matmul(
                ybank[:], identbf[:], p2_h[hb][:],
                start=(s == 0), stop=(s == DS - 1)),
                waits=[("sV", svh), ("sV", ybank_free[ct % 2])], inc="sY")
            hpar[hb] = p.cur("sY")
        sy = p.cur("sY")
        y1w = [("sY", sy), ("dS", uld[j])]
        if ct == 0:
            y1w.append(("dG", y1sp["dg"]))
        p.emit("vector", lambda e, ct=ct, ybank=ybank: e.scalar_tensor_tensor(
            p2_y1[:, ct * TT:(ct + 1) * TT], p2_u[:, ct * TT:(ct + 1) * TT],
            Dv[:, ct:ct + 1], ybank[:], Alu.mult, Alu.add),
            waits=y1w, inc="sV")
        ybank_free[ct % 2] = p.cur("sV")
        y1_done[(j, ct)] = p.cur("sV")
        if ct == 15:
            scan_done[j] = p.cur("sV")
            y1_done[j] = p.cur("sV")

    w2_done = {}

    def ffn_tail(j):
        for mo in range(KD):
            w2w = [("dG", pre_w2), ("sT", acc_done.get((j, mo - 2, "t"), 0))]
            if mo < 2:
                w2w.append(("sT", p.cur("sT")))
            p.emit("gpsimd", lambda e, mo=mo: e.dma_start(
                out=p2_w2[mo % 2][:], in_=wbf_w2[mo]), waits=w2w, inc="dG", incby=16)
            wd = p.cur("dG")
            mw = [("dG", wd), ("sV", actq_done[j]), ("dS", xld[j]),
                  ("sS", acc_done.get((j, mo - 1, "e"), 0))]
            for k in range(KFF):
                p.emit("tensor", lambda e, mo=mo, k=k: e.matmul(
                    PS[2][:], p2_w2[mo % 2][:, k * 128:(k + 1) * 128],
                    p2_actq[:, k * TT:(k + 1) * TT],
                    start=(k == 0), stop=(k == KFF - 1)),
                    waits=mw if k == 0 else (), inc="sT")
            st = p.cur("sT")
            ach = p2_at[mo % 2]
            p.emit("scalar", lambda e, mo=mo, ach=ach: e.activation(
                ach[:], PS[2][:], Act.Identity, bias=b2[:, mo:mo + 1]),
                waits=[("sT", st), ("sV", acc_done.get((j, mo - 2, "v"), 0))], inc="sS")
            acc_done[(j, mo, "e")] = p.cur("sS")
            ss = p.cur("sS")
            ps7bf = PS[7][:].bitcast(BF16)
            for tb in range(4):
                tw = [("sS", ss)]
                if tb == 0:
                    tw.append(("sV", p.cur("sV") if mo == 0 else
                               acc_done.get((j, mo - 1, "v"), 0)))
                p.emit("tensor", lambda e, tb=tb, ach=ach, ps7bf=ps7bf: e.transpose(
                    ps7bf[:, tb * 128:(tb + 1) * 128],
                    ach[:, tb * 128:(tb + 1) * 128], identbf[:]),
                    waits=tw if tb == 0 else (), inc="sT")
            st = p.cur("sT")
            acc_done[(j, mo, "t")] = st
            accv = p2_acc.rearrange("p (b f) -> p b f", f=D)
            p.emit("vector", lambda e, mo=mo, accv=accv, ps7bf=ps7bf: e.tensor_tensor(
                accv[:, :, mo * 128:(mo + 1) * 128],
                accv[:, :, mo * 128:(mo + 1) * 128],
                ps7bf[:, 0:512].rearrange("p (b f) -> p b f", f=128),
                Alu.add), waits=[("sT", st), ("dS", xld[j])], inc="sV")
            acc_done[(j, mo, "v")] = p.cur("sV")
        acc_done[j] = p.cur("sV")
        w2_done[j] = p.cur("sT")

    def epilogue(j, skip_y2=False):
        if not skip_y2:
            for ct in range(NCT):
                p.emit("vector", lambda e, ct=ct: e.tensor_tensor(
                    p2_sz[:, ct * TT:(ct + 1) * TT], p2_y1[:, ct * TT:(ct + 1) * TT],
                    p2_sz[:, ct * TT:(ct + 1) * TT], Alu.mult),
                    waits=[("dS", szld[j]), ("sV", y1_done[j])] if ct == 0 else (),
                    inc="sV")
        y2_done[j] = p.cur("sV")
        sv = p.cur("sV")
        for mo in range(KD):
            wow = [("dG", pre_wo), ("sT", out_done.get((j, mo - 2, "t"), 0))]
            if mo < 2:
                wow.append(("sT", p.cur("sT")))
            p.emit("gpsimd", lambda e, mo=mo: e.dma_start(
                out=p2_wo[mo % 2][:], in_=wbf_wo[mo]), waits=wow, inc="dG", incby=16)
            wd = p.cur("dG")
            mw = [("dG", wd), ("sV", sv), ("sS", out_done.get((j, mo - 1, "e"), 0))]
            for k in range(NCT):
                p.emit("tensor", lambda e, mo=mo, k=k: e.matmul(
                    PS[4][:], p2_wo[mo % 2][:, k * 128:(k + 1) * 128],
                    p2_sz[:, k * TT:(k + 1) * TT],
                    start=(k == 0), stop=(k == NCT - 1)),
                    waits=mw if k == 0 else (), inc="sT")
            st = p.cur("sT")
            ach = p2_at[mo % 2]
            p.emit("scalar", lambda e, mo=mo, ach=ach: e.activation(
                ach[:], PS[4][:], Act.Identity, bias=bout[:, mo:mo + 1]),
                waits=[("sT", st), ("sV", out_done.get((j, mo - 2, "v"), 0))], inc="sS")
            out_done[(j, mo, "e")] = p.cur("sS")
            ss = p.cur("sS")
            ps7bf = PS[7][:].bitcast(BF16)
            for tb in range(4):
                tw = [("sS", ss)]
                if tb == 0:
                    tw.append(("sV", acc_done[j] if mo == 0 else
                               out_done.get((j, mo - 1, "v"), 0)))
                p.emit("tensor", lambda e, tb=tb, ach=ach, ps7bf=ps7bf: e.transpose(
                    ps7bf[:, tb * 128:(tb + 1) * 128],
                    ach[:, tb * 128:(tb + 1) * 128], identbf[:]),
                    waits=tw if tb == 0 else (), inc="sT")
            st = p.cur("sT")
            out_done[(j, mo, "t")] = st
            accv = p2_acc.rearrange("p (b f) -> p b f", f=D)
            p.emit("vector", lambda e, mo=mo, accv=accv, ps7bf=ps7bf: e.tensor_tensor(
                accv[:, :, mo * 128:(mo + 1) * 128],
                accv[:, :, mo * 128:(mo + 1) * 128],
                ps7bf[:, 0:512].rearrange("p (b f) -> p b f", f=128),
                Alu.add), waits=[("sT", st), ("sV", acc_done[j])], inc="sV")
            out_done[(j, mo, "v")] = p.cur("sV")
        sv = p.cur("sV")
        for b in range(4):
            p.emit("sync", lambda e, j=j, b=b: e.dma_start(
                out=o_out[j * TT + b * 128: j * TT + (b + 1) * 128, :],
                in_=p2_acc[:, b * D:(b + 1) * D]),
                waits=[("sV", sv)], inc="dS", incby=16)
        out_done[j] = p.cur("dS")

    def corr_spills(j):
        p.emit("gpsimd", lambda e, j=j: e.dma_start(out=sp_y1[j], in_=p2_y1[:, :]),
               waits=[("sV", y1_done[j])], inc="dG", incby=16)
        y1sp["dg"] = p.cur("dG")
        p.emit("gpsimd", lambda e, j=j: e.dma_start(out=sp_acc[j], in_=p2_acc[:]),
               waits=[("sV", acc_done[j])], inc="dG", incby=16)
        accsp["dg"] = p.cur("dG")
        y2_done[j] = p.cur("sV")
        out_done[j] = 0

    for j in range(NT):
        if j >= 1:
            ffn_tail(j - 1)
            if j - 1 >= T_CORR_TILES:
                epilogue(j - 1)
            else:
                corr_spills(j - 1)
        reloads(j)
        bbc_ready = bbc_compute(j)
        for ct in range(NCT):
            ffn_piece(j, ct)
        for ct in range(NCT):
            scan_ct(j, ct, bbc_ready)
    ffn_tail(NT - 1)
    epilogue(NT - 1)

    p.emit("sync", lambda e: e.dma_start(out=exch_in[:, 0:NCT * DS], in_=aend[:]),
           waits=[("sS", aend_done)], inc="dS", incby=16)
    p.emit("sync", lambda e: e.dma_start(out=exch_in[:, NCT * DS:], in_=hcar[:]),
           waits=[("sV", scan_done[NT - 1])], inc="dS", incby=16)
    exin = p.cur("dS")
    p.emit("gpsimd", lambda e: e.collective_compute(
        "AllGather", Alu.bypass,
        replica_groups=[[0, 1, 2, 3], [4, 5, 6, 7]],
        ins=[exch_in_t.ap()],
        outs=[exch_all_t.ap()]), waits=[("dS", exin)], inc="sG")
    scoll = p.cur("sG")
    exw = [("sG", scoll), ("sV", p.cur("sV")), ("sS", p.cur("sS")),
           ("sT", p.cur("sT"))]
    for i in range(4):
        p.emit("sync", lambda e, i=i: e.dma_start(
            out=p3_exg[:, i * 512:(i + 1) * 512], in_=exch_all[i]),
            waits=exw, inc="dS", incby=16)
    exg_ld = p.cur("dS")

    NS = NCT * DS

    def hf(fn):
        p.emit("vector", fn, waits=[("sV", p.cur("sV"))], inc="sV")

    p.emit("vector", lambda e: e.memset(p3_H[:, 0:NS], 0.0),
           waits=[("dS", exg_ld), ("sV", p.cur("sV"))], inc="sV")
    p.emit("vector", lambda e: e.memset(p3_Gr[:, 0:NCT], 0.0), inc="sV")
    hf(lambda e: e.tensor_scalar(p3_on[:, 0:4], pmask[:], -1.0, 1.0, Alu.mult, Alu.add))
    for i in range(3):
        hf(lambda e, i=i: e.scalar_tensor_tensor(
            p3_tmp[:, 0:NS], p3_exg[:, i * 512: i * 512 + NS], pmask[:, i:i + 1],
            p3_on[:, i:i + 1].broadcast_to([128, NS]), Alu.mult, Alu.add))
        hf(lambda e: e.tensor_tensor(p3_H[:, 0:NS], p3_H[:, 0:NS], p3_tmp[:, 0:NS],
                                     Alu.mult))
        hf(lambda e, i=i: e.tensor_scalar(
            p3_tmp[:, 0:NS], p3_exg[:, i * 512 + NS: (i + 1) * 512], pmask[:, i:i + 1],
            None, Alu.mult))
        hf(lambda e: e.tensor_tensor(p3_H[:, 0:NS], p3_H[:, 0:NS], p3_tmp[:, 0:NS],
                                     Alu.add))
    Wp3 = p.snap()

    for j in range(T_CORR_TILES):
        cw_ = [("dG", accsp["dg"]), ("dG", y1sp["dg"]),
               ("sV", p.cur("sV")), ("sS", p.cur("sS")), ("sT", p.cur("sT")),
               ("dS", out_done[NT - 1])]
        p.emit("sync", lambda e, j=j: e.dma_start(out=p2_d, in_=sp_delta[j]),
               waits=cw_, inc="dS", incby=16)
        p.emit("sync", lambda e, j=j: e.dma_start(out=p2_y1, in_=sp_y1[j]),
               waits=cw_, inc="dS", incby=16)
        p.emit("sync", lambda e, j=j: e.dma_start(out=p2_sz, in_=sp_sz[j]),
               waits=cw_, inc="dS", incby=16)
        p.emit("sync", lambda e, j=j: e.dma_start(out=p2_acc[:], in_=sp_acc[j]),
               waits=cw_, inc="dS", incby=16)
        cld = p.cur("dS")

        for ct in range(NCT):
            gw = [("dS", cld)] + (Wp3 if ct == 0 else [("sS", p.cur("sS"))])
            p.emit("vector", lambda e, ct=ct: e.tensor_tensor_scan(
                p3_G[:, 0:TT], p2_d[:, ct * TT:(ct + 1) * TT],
                p2_d[:, ct * TT:(ct + 1) * TT],
                p3_Gr[:, ct:ct + 1], Alu.bypass, Alu.add), waits=gw, inc="sV")
            p.emit("vector", lambda e, ct=ct: e.tensor_scalar(
                p3_Gr[:, ct:ct + 1], p3_G[:, TT - 1:TT], 1.0, None, Alu.mult),
                waits=[("sV", p.cur("sV"))], inc="sV")
            p.emit("vector", lambda e, ct=ct: e.tensor_copy(
                p3_y1f[:, 0:TT], p2_y1[:, ct * TT:(ct + 1) * TT]), inc="sV")
            for s in range(DS):
                idx = ct * DS + s
                sv = p.cur("sV")
                p.emit("scalar", lambda e, idx=idx: e.activation(
                    p3_E[:, 0:TT], p3_G[:, 0:TT], Act.Exp,
                    scale=Asb[:, idx:idx + 1]), waits=[("sV", sv)], inc="sS")
                ss = p.cur("sS")
                p.emit("vector", lambda e, idx=idx: e.scalar_tensor_tensor(
                    p3_y1f[:, 0:TT], p3_E[:, 0:TT], p3_H[:, idx:idx + 1],
                    p3_y1f[:, 0:TT], Alu.mult, Alu.add), waits=[("sS", ss)], inc="sV")
            p.emit("vector", lambda e, ct=ct: e.tensor_tensor(
                p2_sz[:, ct * TT:(ct + 1) * TT], p3_y1f[:, 0:TT],
                p2_sz[:, ct * TT:(ct + 1) * TT], Alu.mult), inc="sV")
        y1_done[j] = p.cur("sV")
        acc_done[j] = p.cur("sV")
        szld[j] = cld
        out_done.pop(j, None)
        epilogue(j, skip_y2=True)

    fin = [("dS", p.cur("dS")), ("dG", p.cur("dG"))]
    p.emit("gpsimd", lambda e: e.memset(lnc[:, 15:16], 0.0), waits=fin)

    with es, semstack, nc.Block() as block:
        p.realize(block, sems)
    return nc



def _shared_consts(inputs):
    f = np.float32
    cc = {}
    cc["W_in"] = np.ascontiguousarray(inputs["L0_attn_W_in"], f)
    cc["W_dt"] = np.ascontiguousarray(inputs["L0_attn_W_dt"], f)
    cc["W_x16"] = np.ascontiguousarray(inputs["L0_attn_W_x"][:, DS:], f)
    cc["W1"] = np.ascontiguousarray(inputs["L0_ffn_W1"], f)
    cc["W2"] = np.ascontiguousarray(inputs["L0_ffn_W2"], f)
    cc["W_out"] = np.ascontiguousarray(inputs["L0_attn_W_out"], f)
    b_in = inputs["L0_attn_b_in"].astype(f)
    cc["b_in_h"] = np.ascontiguousarray(b_in[:DI].reshape(NCT, 128).T)
    cc["b_in_z"] = np.ascontiguousarray(b_in[DI:].reshape(NCT, 128).T)
    cc["b_dt"] = np.ascontiguousarray(inputs["L0_attn_b_dt"].astype(f).reshape(NCT, 128).T)
    bx = np.zeros((128, 1), f)
    bx[:DS, 0] = inputs["L0_attn_b_x"].astype(f)[DS:]
    cc["b_x16"] = bx
    cc["conv_w"] = np.ascontiguousarray(
        inputs["L0_attn_conv_w"].astype(f)[:, 0, :].reshape(NCT, 128, 4)
        .transpose(1, 0, 2).reshape(128, NCT * 4))
    cc["conv_b"] = np.ascontiguousarray(inputs["L0_attn_conv_b"].astype(f).reshape(NCT, 128).T)
    cc["A_log"] = np.ascontiguousarray(
        inputs["L0_attn_A_log"].astype(f).reshape(NCT, 128, DS)
        .transpose(1, 0, 2).reshape(128, NCT * DS))
    cc["Dv"] = np.ascontiguousarray(inputs["L0_attn_D"].astype(f).reshape(NCT, 128).T)
    cc["b1"] = np.ascontiguousarray(inputs["L0_ffn_b1"].astype(f).reshape(KFF, 128).T)
    cc["theta"] = np.ascontiguousarray(
        inputs["L0_ffn_quantum_phases"].astype(f).reshape(KFF, 128).T)
    cc["wreal"] = np.ascontiguousarray(
        inputs["L0_ffn_quantum_weights_real"].astype(f).reshape(KFF, 128).T)
    cc["wimag"] = np.ascontiguousarray(
        inputs["L0_ffn_quantum_weights_imag"].astype(f).reshape(KFF, 128).T)
    cc["b2"] = np.ascontiguousarray(inputs["L0_ffn_b2"].astype(f).reshape(KD, 128).T)
    cc["b_out"] = np.ascontiguousarray(
        inputs["L0_attn_b_out"].astype(f).reshape(KD, 128).T)
    cc["identbf"] = np.eye(128, dtype=ml_dtypes.bfloat16)
    sels = np.zeros((DS, DS * 128), dtype=ml_dtypes.bfloat16)
    for s in range(DS):
        sels[s, s * 128:(s + 1) * 128] = 1
    cc["sels"] = sels
    return cc


_NC_CACHE = []


def kernel(**inputs):
    f = np.float32
    x = np.ascontiguousarray(inputs["L0_attn_x"], f).reshape(2 * 16384, D)
    cc = _shared_consts(inputs)

    if not _NC_CACHE:
        _NC_CACHE.append(build_fused())
    nc = _NC_CACHE[0]
    in_maps = []
    for k in range(NCORES):
        sq = k % 4
        m = dict(cc)
        m["x"] = np.ascontiguousarray(x[k * TCORE:(k + 1) * TCORE])
        m["xhalo"] = (np.ascontiguousarray(x[k * TCORE - 128:k * TCORE])
                      if sq > 0 else np.zeros((128, D), f))
        m["halo_mask"] = np.full((128, 1), 1.0 if sq > 0 else 0.0, f)
        pm = np.zeros((128, 4), f)
        pm[:, :sq] = 1.0
        m["pmask"] = pm
        in_maps.append(m)
    res = run_bass_kernel_spmd(nc, in_maps, core_ids=list(range(NCORES)))
    out = np.concatenate(
        [np.asarray(res.results[k]["o_out"], f) for k in range(NCORES)], axis=0)
    return out.reshape(2, 16384, D)
